# version 35
# baseline (speedup 1.0000x reference)
import numpy as np
import concourse.bass as bass
import concourse.mybir as mybir
from concourse.bass_utils import run_bass_kernel_spmd

F32 = mybir.dt.float32
BF16 = mybir.dt.bfloat16
U8 = mybir.dt.uint8
AF = mybir.ActivationFunctionType
ALU = mybir.AluOpType
AX = mybir.AxisListType

D = 1024
NCH = 8
QBF = 128.0
CLIP_HI = float(np.float32(QBF - 1e-5))
CLIP_LO = float(np.float32(-QBF + 1e-5))
ATT_SCALE = 0.03125
EPS_OUT = 1e-5
EPS_IN = 1e-3
EPS_W = 1e-5


class E:

    def __init__(self, sem):
        self.sem = sem
        self.n = 0
        self.seen = {}
        self.h = None

    def w(self, tok):
        if tok is None:
            return
        sem, val = tok
        if self.seen.get(sem.name, 0) >= val:
            return
        self.h.wait_ge(sem, val)
        self.seen[sem.name] = val

    def wmany(self, toks):
        for t in toks:
            self.w(t)

    def d(self, inst):
        inst.then_inc(self.sem, 1)
        self.n += 1
        return (self.sem, self.n)

    def t(self):
        return (self.sem, self.n)


class Buf:

    def __init__(self, name):
        self.name = name
        self.lw = None
        self.lr = []

    def rd_waits(self):
        return [self.lw]

    def wr_waits(self):
        return ([self.lw] if self.lw else []) + list(self.lr)

    def note_rd(self, tok):
        self.lr.append(tok)

    def note_wr(self, tok):
        self.lw = tok
        self.lr = []


def build_nc(B=2, T=2048, debug=False):
    NT = T // 128
    NSL = T // 512
    nc = bass.Bass("TRN2", target_bir_lowering=False)

    x_in = nc.dram_tensor("x", [B, T, D], F32, kind="ExternalInput").ap()
    out_t = nc.dram_tensor("out", [B, T, D], F32, kind="ExternalOutput").ap()
    w_hbm = {}
    for nm in ("w1", "w2", "w3", "w4", "wf"):
        w_hbm[nm] = nc.dram_tensor(nm, [D, D], F32, kind="ExternalInput").ap()
    vecs = {}
    for nm in ("b1", "b2", "b3", "b4", "gam", "bet"):
        vecs[nm] = nc.dram_tensor(nm, [D], F32, kind="ExternalInput").ap()
    spill = nc.dram_tensor("spill", [B, T, D], F32).ap()
    xnsp = nc.dram_tensor("xnsp", [B, T, D], BF16).ap()
    sc128 = nc.dram_tensor("sc128", [128], F32).ap()
    sc128b = nc.dram_tensor("sc128b", [128], F32).ap()
    sc1a = nc.dram_tensor("sc1a", [1, 1], F32).ap()
    lrowb = nc.dram_tensor("lrowb_d", [1, 1], F32).ap()
    cc_in = nc.dram_tensor("cc_in", [1, 1], F32).ap()
    cc_out = nc.dram_tensor("cc_out", [1, 1], F32).ap()

    def sb(name, shape, dt):
        return nc.alloc_sbuf_tensor(name, shape, dt).ap()

    QF = sb("QF", [128, NCH * T], BF16)
    KF = sb("KF", [128, NCH * T], BF16)
    VT = sb("VT", [128, NT * D], BF16)
    W4S = sb("W4S", [128, NCH * D], BF16)
    WQS = sb("WQS", [128, NCH * D], BF16)
    idn_bf = sb("idn_bf", [128, 128], BF16)
    b1c = sb("b1c", [128, NCH], F32)
    b2c = sb("b2c", [128, NCH], F32)
    lrow = sb("lrow", [1, 128], F32)
    l11 = sb("l11", [1, 1], F32)
    lrowB = sb("lrowB", [1, 128], F32)
    l11b = sb("l11b", [1, 1], F32)
    TINY = sb("TINY", [128, 64], F32)
    tiny_i = [0]

    def tiny():
        c = tiny_i[0]
        tiny_i[0] += 1
        assert c < 64
        return TINY[:, c:c + 1]

    rbeta_bc = tiny()
    betaw_bc = tiny()
    gam_g = tiny()
    s_g = tiny()
    gb_g = tiny()
    gmax = [tiny(), tiny()]

    ARENA_BYTES = 75776
    arena = sb("arena", [128, ARENA_BYTES], U8)

    def ar(off, nelem, dt, width):
        return arena[:, off:off + nelem * width].bitcast(dt)

    def arf(off, n):
        return ar(off, n, F32, 4)

    def arb(off, n):
        return ar(off, n, BF16, 2)

    pS0 = arf(0, 4096)
    pS1 = arf(16384, 4096)
    pTH = arf(32768, 4096)
    pRH = arf(49152, 4096)
    pIOTA = arf(73728, 128)
    idn_f = arf(74240 // 4 * 0 + 74240, 128)

    LN1F = arb(0, 16384)
    pXT = [arf(32768, 1024), arf(36864, 1024), arf(40960, 1024)]
    pSCR = [arf(45056, 1024)]
    pLNT = [arb(49152, 1024), arb(51200, 1024)]
    pB3 = arf(53248, 1024)
    WBUF = arb(57344, 8192)

    lP = arb(0, 2048)
    lPT = arb(4096, 2048)
    lOT = arb(8192, 1024)
    lOF = arb(10240, 1024)
    lZ = arf(12288, 1024)
    lXT = [arf(16384, 1024), arf(20480, 1024)]
    lX2 = [arf(24576, 1024), arf(28672, 1024)]
    lA = [arf(32768, 1024), arf(36864, 1024)]
    lBu = [arf(40960, 1024), arf(45056, 1024)]
    lB4 = arf(49152, 1024)
    lGAMb = arb(53248, 1024)
    lBETb = arb(55296, 1024)
    lW16 = [arb(40960, 1024), arb(45056, 1024)]
    lXN1 = [arb(32768, 1024), arb(36864, 1024)]
    lXN2 = [arb(40960 + 2048, 1024), arb(45056 + 2048, 1024)]

    fX2 = [arf(0, 1024), arf(4096, 1024)]
    cXB = [arb(8192, 1024), arb(10240, 1024),
           arb(61440, 1024), arb(63488, 1024)]
    fXQF = [arb(16384, 1024), arb(18432, 1024)]
    fY = [arf(20480, 1024), arf(24576, 1024)]
    fOU = [arf(28672, 1024), arf(32768, 1024)]
    fA = [arf(36864, 1024), arf(40960, 1024)]
    fB = [arf(45056, 1024), arf(49152, 1024)]
    fX2b = [arf(53248, 1024), arf(57344, 1024)]
    fYb = [arf(65536, 1024), arf(69632, 1024)]
    cXN = [arb(65536, 1024), arb(67584, 1024),
           arb(69632, 1024), arb(71680, 1024)]
    fXQFa = [arb(16384, 1024), arb(18432, 1024),
             arb(12288, 1024), arb(14336, 1024)]

    psS = nc.psum_tensor("psS", [128, 2048], F32).__enter__()
    psA = nc.psum_tensor("psA", [128, 1024], F32).__enter__()
    psB = nc.psum_tensor("psB", [128, 1024], F32).__enter__()
    psS0 = psS[:, 0:1024]
    psS1 = psS[:, 1024:2048]
    vLT0 = psS[:, 0:512].bitcast(BF16)
    vLT1 = psS[:, 1024:1536].bitcast(BF16)
    vPT = psA[:, :].bitcast(BF16)
    vOF = psB[:, 0:512].bitcast(BF16)

    def sem(n):
        return nc.semaphore(n).__enter__()

    PE = E(sem("s_pe"))
    VE = E(sem("s_ve"))
    SE = E(sem("s_se"))
    GE = E(sem("s_ge"))
    SY = E(sem("s_sy"))
    ENGS = {"tensor": PE, "vector": VE, "scalar": SE, "gpsimd": GE, "sync": SY}

    class DS:
        def __init__(self, name):
            self.sem = sem(name)
            self.n = 0

        def go(self, q, out, in_, **kw):
            q.h.dma_start(out=out, in_=in_, **kw).then_inc(self.sem, 16)
            self.n += 1
            return (self.sem, self.n * 16)

    dx = [DS("d_x0"), DS("d_x1"), DS("d_x2"), DS("d_x3")]
    dxn = [DS("d_xn0"), DS("d_xn1"), DS("d_xn2"), DS("d_xn3")]
    dw = {nm: DS("d_" + nm) for nm in ("w1", "w2", "w3")}
    dwc = [DS(f"d_wc{c}") for c in range(NCH)]
    dwf = [DS(f"d_wf{c}") for c in range(NCH)]
    dxb = [DS(f"d_xb{c}") for c in range(4)]
    dgm = DS("d_gm")
    dpm = DS("d_pm")
    dbc = DS("d_bc")
    dsp = [DS("d_sp0"), DS("d_sp1")]
    dxs = [DS("d_xs0"), DS("d_xs1")]
    dou = [DS("d_ou0"), DS("d_ou1")]
    dcc = DS("d_cc")
    dcc2 = DS("d_cc2")
    ccs = sem("s_cc")

    bufs = {}

    def bget(name):
        if name not in bufs:
            bufs[name] = Buf(name)
        return bufs[name]

    cur_blk = [None]

    def sec(ename, body):
        eng = ENGS[ename]
        getattr(cur_blk[0], ename)(lambda h: (setattr(eng, "h", h), body(h))[0])

    def op(ename, reads, writes, emit):
        eng = ENGS[ename]
        tok_out = [None]

        def body(h):
            for r in reads:
                eng.wmany(bget(r).rd_waits())
            for wn in writes:
                eng.wmany(bget(wn).wr_waits())
            tok = eng.d(emit(h))
            for r in reads:
                bget(r).note_rd(tok)
            for wn in writes:
                bget(wn).note_wr(tok)
            tok_out[0] = tok

        sec(ename, body)
        return tok_out[0]

    def dma(qname, stream, reads, writes, out, in_, **kw):
        eng = ENGS[qname]
        tok_out = [None]

        def body(h):
            for r in reads:
                eng.wmany(bget(r).rd_waits())
            for wn in writes:
                eng.wmany(bget(wn).wr_waits())
            tok = stream.go(eng, out, in_, **kw)
            for r in reads:
                bget(r).note_rd(tok)
            for wn in writes:
                bget(wn).note_wr(tok)
            tok_out[0] = tok

        sec(qname, body)
        return tok_out[0]

    def dma_batch(qname, stream, entries):
        eng = ENGS[qname]

        def body(h):
            for reads, writes, out, in_, kw in entries:
                for r in reads:
                    eng.wmany(bget(r).rd_waits())
                for wn in writes:
                    eng.wmany(bget(wn).wr_waits())
                stream.go(eng, out, in_, **kw)
            tok = (stream.sem, stream.n * 16)
            for reads, writes, out, in_, kw in entries:
                for r in reads:
                    bget(r).note_rd(tok)
                for wn in writes:
                    bget(wn).note_wr(tok)

        sec(qname, body)

    eps_ap = {}

    def rstd(vsum, n, eps, tag):
        lt = tiny_pool(tag + "_lt")
        r = tiny_pool(tag + "_r")
        op("scalar", [tag + "_acc", "eps_c"], [tag + "_lt"],
           lambda h: h.activation(lt, vsum, AF.Ln, scale=1.0 / n,
                                  bias=eps_ap[eps]))
        op("scalar", [tag + "_lt"], [tag + "_r"],
           lambda h: h.activation(r, lt, AF.Exp, scale=-0.5))
        return r

    from concourse.hw_specs import get_activation_tables
    _act_set = list(get_activation_tables(nc.m.arch)).index(
        "natural_log_exp_and_others")

    def preload_tables():
        def body(h):
            h.add_instruction(mybir.InstLoadActFuncSet(
                name=nc.get_next_instruction_name(),
                act_func_set_id=_act_set, ins=[], outs=[]))

        sec("scalar", body)

    tiny_slots = {}

    def tiny_pool(tag):
        if tag not in tiny_slots:
            tiny_slots[tag] = tiny()
        return tiny_slots[tag]

    ones_like = None

    def wchunk_load(wname):
        for c in range(NCH):
            dma("gpsimd", dwc[c], [f"WB{c}"], [f"WB{c}"],
                WBUF[:, c * 1024:(c + 1) * 1024],
                w_hbm[wname][c * 128:(c + 1) * 128, :])

    def emit_prologue():

            entries = [
                ([], ["W4S"], W4S.rearrange("p (c u) -> p c u", c=NCH),
                 w_hbm["w4"].rearrange("(c p) u -> p c u", p=128), {}),
            ]
            for c in range(NCH):
                entries.append(([], ["b1c"], b1c[:, c:c + 1],
                                vecs["b1"][c * 128:(c + 1) * 128].rearrange(
                                    "(p o) -> p o", o=1), {}))
                entries.append(([], ["b2c"], b2c[:, c:c + 1],
                                vecs["b2"][c * 128:(c + 1) * 128].rearrange(
                                    "(p o) -> p o", o=1), {}))
            dma_batch("gpsimd", dpm, entries)
            for v in (EPS_OUT, EPS_IN):
                eps_ap[v] = tiny()
                op("vector", [], ["eps_c"],
                   lambda h, v=v: h.memset(eps_ap[v], v))
            op("gpsimd", [], ["iota"],
               lambda h: h.iota(pIOTA, [[1, 128]], base=0, channel_multiplier=-1,
                                allow_small_or_imprecise_dtypes=True))
            op("vector", ["iota"], ["idn_bf"],
               lambda h: h.tensor_scalar(idn_bf[:, :], pIOTA, 0.0, None, ALU.is_equal))
            op("vector", ["iota"], ["idn_f"],
               lambda h: h.tensor_scalar(idn_f[:, :], pIOTA, 0.0, None, ALU.is_equal))



    def phase_P(b, w1_prefetched=False):
        with nc.Block() as blk:
            cur_blk[0] = blk
            preload_tables()
            if b == 0:
                emit_prologue()
            dma_batch("gpsimd", dbc, [([], ["pB3"], pB3,
                       vecs["b3"].unsqueeze(0).partition_broadcast(128), {})])
            if not w1_prefetched:
                wchunk_load("w1")

            slots = [(psA, 0, "psA0"), (psA, 512, "psA1"),
                     (psB, 0, "psB0"), (psB, 512, "psB1")]

            def emit_tile(i):
                p = i % 2
                px = i % 3
                xb, sc = f"pXT{px}", "pSCR0"
                dma("sync", dx[px], [], [xb], pXT[px],
                    x_in[b, i * 128:(i + 1) * 128, :])
                s1 = tiny_pool("P_s1")
                op("scalar", [xb], [sc, "P_s1"],
                   lambda h: h.activation(pSCR[0], pXT[px], AF.Identity,
                                          accum_out=s1))
                mu1 = tiny_pool("P_mu1")
                op("vector", ["P_s1"], ["P_mu1"],
                   lambda h: h.tensor_scalar(mu1, s1, 1.0 / D, None, ALU.mult))
                nm1 = tiny_pool("P_nm1")
                op("vector", ["P_s1"], ["P_nm1"],
                   lambda h: h.tensor_scalar(nm1, s1, -1.0 / D, None, ALU.mult))
                v1 = tiny_pool("P_v1")
                op("scalar", [xb, "P_nm1"], [sc, "P_acc"],
                   lambda h: h.activation(pSCR[0], pXT[px], AF.Square, bias=nm1,
                                          accum_out=v1))
                r1 = rstd(v1, D, EPS_OUT, "P")
                lt = f"pLNT{p}"
                op("vector", [xb, "P_mu1", "P_r"], [lt],
                   lambda h: h.tensor_scalar(pLNT[p], pXT[px], mu1, r1,
                                             ALU.subtract, ALU.mult))
                vlt = vLT0 if p == 0 else vLT1
                psn = f"psS{p}"

                def tr(h, vlt=vlt, p=p):
                    for c in range(NCH):
                        last = h.transpose(vlt[:, c * 128:(c + 1) * 128],
                                           pLNT[p][:, c * 128:(c + 1) * 128],
                                           idn_bf[:, :])
                    return last

                op("tensor", [lt, "idn_bf"], [psn], tr)
                op("vector", [psn], ["LN1F"],
                   lambda h, vlt=vlt, i=i: h.tensor_copy(
                       LN1F[:, 0:NCH * T].rearrange(
                           "p (c t) -> p c t", c=NCH)[
                           :, :, i * 128:(i + 1) * 128],
                       vlt.rearrange("p (c t) -> p c t", c=NCH)))

            def proj_pass(groups, kind, dst_ap=None, dst_name=None,
                          bias_c=None, bias_name=None):
                gs = [(g, slots[j]) for j, g in enumerate(groups)]
                for g, (ps, of, pn) in gs:
                    def mm(h, g=g, ps=ps, of=of):
                        for dc in range(NCH):
                            if kind == "fm":
                                k, ec = g
                                last = h.matmul(
                                    ps[:, of:of + 512],
                                    WBUF[:, dc * 1024 + ec * 128:dc * 1024 + (ec + 1) * 128],
                                    LN1F[:, dc * T + k * 512:dc * T + (k + 1) * 512],
                                    start=(dc == 0), stop=(dc == NCH - 1))
                            else:
                                tc, uh = g
                                last = h.matmul(
                                    ps[:, of:of + 512],
                                    LN1F[:, dc * T + tc * 128:dc * T + (tc + 1) * 128],
                                    WBUF[:, dc * 1024 + uh * 512:dc * 1024 + (uh + 1) * 512],
                                    start=(dc == 0), stop=(dc == NCH - 1))
                        return last

                    op("tensor", [f"WB{c}" for c in range(NCH)] + ["LN1F"],
                       [pn], mm)
                for g, (ps, of, pn) in gs:
                    if kind == "fm":
                        k, ec = g
                        op("scalar", [pn, bias_name], [dst_name],
                           lambda h, ps=ps, of=of, ec=ec, k=k:
                           h.activation(
                               dst_ap[:, ec * T + k * 512:ec * T + (k + 1) * 512],
                               ps[:, of:of + 512], AF.Identity,
                               bias=bias_c[:, ec:ec + 1]))
                    else:
                        tc, uh = g
                        op("vector", [pn, "pB3"], ["VT"],
                           lambda h, ps=ps, of=of, tc=tc, uh=uh:
                           h.tensor_tensor(
                               VT[:, tc * D + uh * 512:tc * D + (uh + 1) * 512],
                               ps[:, of:of + 512],
                               pB3[:, uh * 512:(uh + 1) * 512], ALU.add))

            qgroups = [(k, ec) for k in range(NSL) for ec in range(NCH)]
            ntile_done = 0
            for j in range(0, len(qgroups), 4):
                need_k = max(k for k, _ in qgroups[j:j + 4])
                while ntile_done < min((need_k + 1) * 4, NT):
                    emit_tile(ntile_done)
                    ntile_done += 1
                proj_pass(qgroups[j:j + 4], "fm", QF, "QF", b1c, "b1c")
            while ntile_done < NT:
                emit_tile(ntile_done)
                ntile_done += 1
            wchunk_load("w2")
            for j in range(0, len(qgroups), 4):
                proj_pass(qgroups[j:j + 4], "fm", KF, "KF", b2c, "b2c")
            wchunk_load("w3")
            vgroups = [(tc, uh) for tc in range(NT) for uh in range(2)]
            for j in range(0, len(vgroups), 4):
                proj_pass(vgroups[j:j + 4], "v")

    gm_tick = [0]

    def phase_L(b):
        with nc.Block() as blk:
            cur_blk[0] = blk
            preload_tables()
            dma_batch("gpsimd", dbc, [
                ([], ["lB4"], lB4,
                 vecs["b4"].unsqueeze(0).partition_broadcast(128), {}),
                ([], ["lGAM"], lGAMb,
                 vecs["gam"].unsqueeze(0).partition_broadcast(128), {}),
                ([], ["lBET"], lBETb,
                 vecs["bet"].unsqueeze(0).partition_broadcast(128), {}),
            ])

            NKH = max(1, T // 1024)
            KW = T // NKH

            def chain_rest(q):
                p = q % 2
                A, Bu = lA[p], lBu[p]
                an, bn = f"lA{p}", f"lBu{p}"
                xn_, x2n = f"lXT{p}", f"lX2{p}"
                sz = tiny_pool("L_sz")
                muz = tiny_pool("L_muz")
                op("vector", ["L_sz"], ["L_muz"],
                   lambda h: h.tensor_scalar(muz, sz, 1.0 / D, None, ALU.mult))
                nmz = tiny_pool("L_nmz")
                op("vector", ["L_sz"], ["L_nmz"],
                   lambda h: h.tensor_scalar(nmz, sz, -1.0 / D, None, ALU.mult))
                vz = tiny_pool("L_vz")
                op("scalar", ["lZ", "L_nmz"], [an, "L_acc"],
                   lambda h: h.activation(A, lZ[:, :], AF.Square, bias=nmz,
                                          accum_out=vz))
                rz = rstd(vz, D, EPS_OUT, "L")
                op("vector", ["lZ", "L_muz", "L_r"], [an],
                   lambda h: h.tensor_scalar(A, lZ[:, :], muz, rz,
                                             ALU.subtract, ALU.mult))
                s2 = tiny_pool("L_s2")
                op("vector", [an, xn_], [x2n, "L_s2"],
                   lambda h: h.scalar_tensor_tensor(
                       lX2[p][:, :], A, 1.0, lXT[p], ALU.bypass, ALU.add,
                       accum_out=s2))
                dma("scalar", dsp[p], [x2n], [],
                    spill[b, q * 128:(q + 1) * 128, :], lX2[p])
                mu2 = tiny_pool("L_mu2")
                op("vector", ["L_s2"], ["L_mu2"],
                   lambda h: h.tensor_scalar(mu2, s2, 1.0 / D, None, ALU.mult))
                nm2 = tiny_pool("L_nm2")
                op("vector", ["L_s2"], ["L_nm2"],
                   lambda h: h.tensor_scalar(nm2, s2, -1.0 / D, None, ALU.mult))
                v2 = tiny_pool("L_v2")
                op("scalar", [x2n, "L_nm2"], [an, "L2_acc"],
                   lambda h: h.activation(A, lX2[p][:, :], AF.Square, bias=nm2,
                                          accum_out=v2))
                r2 = rstd2(v2, D, EPS_OUT, "L2")
                op("vector", [x2n, "L_mu2", "L2_r"], [an],
                   lambda h: h.tensor_scalar(A, lX2[p][:, :], mu2, r2,
                                             ALU.subtract, ALU.mult))
                d3 = tiny_pool("L_d3")
                op("vector", ["L_s2", "L_mu2"], ["L_d3"],
                   lambda h: h.tensor_scalar(d3, s2, 1.0 / D, mu2,
                                             ALU.mult, ALU.subtract))
                mu3 = tiny_pool("L_mu3")
                op("vector", ["L_d3", "L2_r"], ["L_mu3"],
                   lambda h: h.tensor_scalar(mu3, d3, r2, None, ALU.mult))
                nm3 = tiny_pool("L_nm3")
                op("vector", ["L_mu3"], ["L_nm3"],
                   lambda h: h.tensor_scalar(nm3, mu3, -1.0, None, ALU.mult))
                v3 = tiny_pool("L_v3")
                op("scalar", [an, "L_nm3"], [bn, "L3_acc"],
                   lambda h: h.activation(Bu, A, AF.Square, bias=nm3,
                                          accum_out=v3))
                r3 = rstd2(v3, D, EPS_IN, "L3")
                op("vector", [an, "L_mu3", "L3_r"], [bn],
                   lambda h: h.tensor_scalar(lW16[p], A, mu3, r3,
                                             ALU.subtract, ALU.mult))
                op("vector", [bn, "lGAM"], [an],
                   lambda h: h.tensor_tensor(lXN1[p], lW16[p], lGAMb, ALU.mult))
                op("vector", [an, "lBET"], [bn],
                   lambda h: h.tensor_tensor(lXN2[p], lXN1[p], lBETb, ALU.add))
                mxq = tiny_pool("L_mxq")
                op("vector", [bn], ["L_mxq"],
                   lambda h: h.tensor_reduce(mxq, lXN2[p], AX.X, ALU.max,
                                             apply_absolute_value=True))
                gi = gm_tick[0]
                gm_tick[0] += 1
                if gi == 0:
                    op("vector", ["L_mxq"], ["gmax"],
                       lambda h: h.tensor_copy(gmax[0], mxq))
                else:
                    op("vector", ["L_mxq", "gmax"], ["gmax"],
                       lambda h, gi=gi: h.tensor_tensor(
                           gmax[gi % 2], gmax[(gi - 1) % 2], mxq, ALU.max))
                dma("gpsimd", dxs[p], [bn], [],
                    xnsp[b, q * 128:(q + 1) * 128, :], lXN2[p])

            def emit_ofz(q):

                def tro(h):
                    for c in range(NCH):
                        last = h.transpose(vOF[:, c * 128:(c + 1) * 128],
                                           lOT[:, c * 128:(c + 1) * 128],
                                           idn_bf[:, :])
                    return last

                op("tensor", ["lOT", "idn_bf"], ["psB"], tro)
                op("scalar", ["psB"], ["lOF"],
                   lambda h: h.activation(lOF[:, :], vOF, AF.Identity))

                def zmm(h):
                    for uh in range(2):
                        for ec in range(NCH):
                            last = h.matmul(
                                psB[:, uh * 512:(uh + 1) * 512],
                                lOF[:, ec * 128:(ec + 1) * 128],
                                W4S[:, ec * D + uh * 512:ec * D + (uh + 1) * 512],
                                start=(ec == 0), stop=(ec == NCH - 1))
                    return last

                op("tensor", ["lOF", "W4S"], ["psB"], zmm)
                sz = tiny_pool("L_sz")
                op("vector", ["psB", "lB4"], ["lZ", "L_sz"],
                   lambda h: h.scalar_tensor_tensor(
                       lZ[:, :], psB[:, :], 1.0, lB4, ALU.bypass, ALU.add,
                       accum_out=sz))

            def emit_scores(q):
                for half in range(NKH):
                    psn = f"psS{half}"

                    def mm(h, half=half, q=q):
                        for t2 in range(KW // 512):
                            tq = half * (KW // 512) + t2
                            for ec in range(NCH):
                                last = h.matmul(
                                    psS[:, tq * 512:(tq + 1) * 512],
                                    QF[:, ec * T + q * 128:ec * T + (q + 1) * 128],
                                    KF[:, ec * T + tq * 512:ec * T + (tq + 1) * 512],
                                    start=(ec == 0), stop=(ec == NCH - 1))
                        return last

                    op("tensor", ["QF", "KF"], [psn], mm)

            emit_scores(0)
            for q in range(NT):
                p = q % 2
                dma("sync", dx[p], [], [f"lXT{p}"], lXT[p],
                    x_in[b, q * 128:(q + 1) * 128, :])
                ng0 = tiny_pool("L_ng0")
                op("vector", ["psS0", "psS1"], ["L_ng0"],
                   lambda h: h.reduce_max(ng0, psS[:, 0:T], axis=AX.X,
                                          negate=True))
                ngs = tiny_pool("L_ngs")
                op("vector", ["L_ng0"], ["L_ngs"],
                   lambda h: h.tensor_scalar(ngs, ng0, ATT_SCALE, None,
                                             ALU.mult))

                if q > 0:
                    emit_ofz(q - 1)

                se0 = tiny_pool("L_se0")
                op("scalar", ["psS0", "psS1", "L_ngs"], ["lP", "L_se0"],
                   lambda h: h.activation(lP[:, 0:T], psS[:, 0:T], AF.Exp,
                                          scale=ATT_SCALE, bias=ngs,
                                          accum_out=se0))
                rs = tiny_pool("L_rs")
                op("vector", ["L_se0"], ["L_rs"],
                   lambda h: h.reciprocal(rs, se0))

                def trp(h):
                    for j in range(NT):
                        last = h.transpose(vPT[:, j * 128:(j + 1) * 128],
                                           lP[:, j * 128:(j + 1) * 128],
                                           idn_bf[:, :])
                    return last

                op("tensor", ["lP", "idn_bf"], ["psA"], trp)
                op("scalar", ["psA"], ["lPT"],
                   lambda h: h.activation(lPT[:, 0:NT * 128],
                                          vPT[:, 0:NT * 128], AF.Identity))

                if q + 1 < NT:
                    emit_scores(q + 1)

                if q > 0:
                    chain_rest(q - 1)


                def pv(h):
                    for uh in range(2):
                        for tc in range(NT):
                            last = h.matmul(
                                psA[:, uh * 512:(uh + 1) * 512],
                                lPT[:, tc * 128:(tc + 1) * 128],
                                VT[:, tc * D + uh * 512:tc * D + (uh + 1) * 512],
                                start=(tc == 0), stop=(tc == NT - 1))
                    return last

                op("tensor", ["lPT", "VT"], ["psA"], pv)
                op("scalar", ["psA", "L_rs"], ["lOT"],
                   lambda h: h.activation(lOT[:, :], psA[:, :], AF.Identity,
                                          scale=rs))

            emit_ofz(NT - 1)
            chain_rest(NT - 1)

            if b + 1 < B:
                wchunk_load("w1")
            else:
                gfin = gmax[(gm_tick[0] - 1) % 2]
                op("tensor", ["gmax", "idn_f"], ["psS0"],
                   lambda h: h.transpose(psS[0:1, 512:640], gfin,
                                         idn_f[:, :]))
                op("vector", ["psS0"], ["l11"],
                   lambda h: h.tensor_reduce(l11, psS[0:1, 512:640], AX.X,
                                             ALU.max))
                dma("sync", dgm, ["l11"], ["cc_in"], cc_in, l11)

                def coll_issue(h):
                    GE.wmany(bget("cc_in").rd_waits())
                    h.collective_compute(
                        "AllReduce", ALU.max, replica_groups=[list(range(8))],
                        ins=[cc_in.opt()], outs=[cc_out.opt()]).then_inc(ccs, 1)

                sec("gpsimd", coll_issue)

            def drain_sp(h):
                h.wait_ge(dsp[0].sem, dsp[0].n * 16)
                h.wait_ge(dsp[1].sem, dsp[1].n * 16)

            sec("scalar", drain_sp)

            def drain_xs(h):
                h.wait_ge(dxs[0].sem, dxs[0].n * 16)
                h.wait_ge(dxs[1].sem, dxs[1].n * 16)

            sec("gpsimd", drain_xs)

    def rstd2(vsum, n, eps, tag):
        return rstd(vsum, n, eps, tag)

    if debug:
        d_ln = nc.dram_tensor("d_ln", [128, NCH * T], BF16,
                              kind="ExternalOutput").ap()
        d_qf = nc.dram_tensor("d_qf", [128, NCH * T], BF16,
                              kind="ExternalOutput").ap()
        d_vt = nc.dram_tensor("d_vt", [128, NT * D], BF16,
                              kind="ExternalOutput").ap()
    ddma = DS("d_dbg")

    def dump_dbg():
        with nc.Block() as blk:
            cur_blk[0] = blk

            def body(h):
                h.dma_start(out=d_ln, in_=LN1F[:, 0:NCH * T]).then_inc(
                    ddma.sem, 16)
                h.dma_start(out=d_qf, in_=QF).then_inc(ddma.sem, 16)
                h.dma_start(out=d_vt, in_=VT).then_inc(ddma.sem, 16)
                h.wait_ge(ddma.sem, 48)

            sec("sync", body)

    for b in range(B):
        phase_P(b, w1_prefetched=(b > 0))
        if debug and b == 0:
            dump_dbg()
        phase_L(b)

    items_f = [(b, i, b * NT + i) for b in range(B) for i in range(NT)]
    NIT = len(items_f)
    NPRE = 8 if NIT >= 16 else max(1, NIT // 2)

    def xnf(t):
        if t < NT:
            return QF[:, t * 1024:(t + 1) * 1024], "QF"
        return KF[:, (t - NT) * 1024:(t - NT + 1) * 1024], "KF"

    with nc.Block() as blk:
        cur_blk[0] = blk
        preload_tables()

        def cxn_load(t):
            b, i, _ = items_f[t]
            sl = t % 4
            dma("sync" if sl % 2 == 0 else "scalar", dxn[sl], [],
                [f"cXN{sl}"], cXN[sl], xnsp[b, i * 128:(i + 1) * 128, :])

        def cxn_tr(t):
            sl, pc = t % 4, t % 2
            vlt = vLT0 if pc == 0 else vLT1
            psn = f"psS{pc}"

            def tr(h, vlt=vlt, sl=sl):
                for c in range(NCH):
                    last = h.transpose(vlt[:, c * 128:(c + 1) * 128],
                                       cXN[sl][:, c * 128:(c + 1) * 128],
                                       idn_bf[:, :])
                return last

            op("tensor", [f"cXN{sl}", "idn_bf"], [psn], tr)
            v, nm = xnf(t)
            if t % 2 == 0:
                op("scalar", [psn], [nm],
                   lambda h, v=v, vlt=vlt: h.activation(v, vlt, AF.Identity))
            else:
                op("vector", [psn], [nm],
                   lambda h, v=v, vlt=vlt: h.tensor_copy(v, vlt))

        for u in range(min(4, NPRE)):
            cxn_load(u)
        pS0c = pS0.rearrange("p (c u) -> p c u", c=4)
        pS1c = pS1.rearrange("p (c u) -> p c u", c=4)
        for c in range(4):
            dma("sync", dwf[c], [], [f"pS0c{c}"], pS0c[:, c, :],
                w_hbm["wf"][c * 128:(c + 1) * 128, :])
            dma("scalar", dwf[4 + c], [], [f"pS1c{c}"], pS1c[:, c, :],
                w_hbm["wf"][512 + c * 128:512 + (c + 1) * 128, :])

        for t in range(NPRE):
            cxn_tr(t)
            if t + 4 < NPRE:
                cxn_load(t + 4)
        asum = tiny_pool("asum")
        op("vector", [f"pS0c{c}" for c in range(4)], ["asum0"],
           lambda h: h.tensor_reduce(asum, pS0, AX.X, ALU.add,
                                     apply_absolute_value=True))
        a2t = tiny_pool("asum2")
        op("vector", [f"pS1c{c}" for c in range(4)], ["asum1"],
           lambda h: h.tensor_reduce(a2t, pS1, AX.X, ALU.add,
                                     apply_absolute_value=True))
        op("vector", ["asum0", "asum1"], ["asum"],
           lambda h: h.tensor_tensor(asum, asum, a2t, ALU.add))
        dma("gpsimd", dcc2, ["asum"], ["sc128b"],
            sc128b.rearrange("(p o) -> p o", o=1), asum)
        dma("gpsimd", dcc2, ["sc128b"], ["lrowB"], lrowB,
            sc128b.rearrange("(a b) -> a b", a=1))
        op("vector", ["lrowB"], ["l11b"],
           lambda h: h.tensor_reduce(l11b, lrowB, AX.X, ALU.add))
        dma("gpsimd", dcc2, ["l11b"], ["sc1a"], sc1a, l11b)
        dma("gpsimd", dcc2, ["sc1a"], ["betaw"], betaw_bc,
            sc1a.partition_broadcast(128))
        op("vector", ["betaw"], ["betaw"],
           lambda h: h.tensor_scalar(betaw_bc, betaw_bc, 1.0 / (D * D), None,
                                     ALU.mult))
        rb_t = tiny_pool("rb_t")
        op("vector", ["betaw"], ["rb_t"],
           lambda h: h.tensor_scalar(rb_t, betaw_bc, EPS_W, None, ALU.add))
        op("vector", ["rb_t"], ["rbeta"],
           lambda h: h.reciprocal(rbeta_bc, rb_t))

        for hn, S_h in (("h0", pS0), ("h1", pS1)):
            chn = [f"pS0c{c}" if hn == "h0" else f"pS1c{c}" for c in range(4)]
            op("vector", chn + ["rbeta"], ["pTH"],
               lambda h, S_h=S_h: h.tensor_scalar(pTH, S_h, rbeta_bc, None,
                                                  ALU.mult))
            op("vector", ["pTH"], ["pS0" if hn == "h0" else "pS1"],
               lambda h, S_h=S_h: h.tensor_scalar(S_h, pTH, 0.5, None,
                                                  ALU.is_gt))
            op("vector", ["pTH"], ["pRH"],
               lambda h: h.tensor_scalar(pRH, pTH, -0.5, None, ALU.is_lt))
            off = 0 if hn == "h0" else 4096
            op("vector", ["pS0" if hn == "h0" else "pS1", "pRH"], ["WQS"],
               lambda h, S_h=S_h, off=off: h.tensor_tensor(
                   WQS[:, off:off + 4096], S_h, pRH, ALU.subtract))

        def coll_wait(h):
            h.wait_ge(ccs, 1)

        sec("sync", coll_wait)
        dma("sync", dgm, [], ["gam_g"], gam_g,
            cc_out.partition_broadcast(128))
        tg = tiny_pool("C_tg")
        op("vector", ["gam_g"], ["C_tg"],
           lambda h: h.reciprocal(tg, gam_g))
        op("vector", ["C_tg"], ["s_g"],
           lambda h: h.tensor_scalar(s_g, tg, QBF, None, ALU.mult))
        op("vector", ["gam_g", "betaw"], ["gb_g"],
           lambda h: h.tensor_scalar(gb_g, gam_g, betaw_bc, 1.0 / QBF,
                                     ALU.mult, ALU.mult))

    with nc.Block() as blk:
        cur_blk[0] = blk
        preload_tables()

        def wsp(h):
            h.wait_ge(dsp[0].sem, dsp[0].n * 16)
            h.wait_ge(dsp[1].sem, dsp[1].n * 16)
            h.wait_ge(dxs[0].sem, dxs[0].n * 16)
            h.wait_ge(dxs[1].sem, dxs[1].n * 16)

        sec("sync", wsp)

        tick = [0]
        fX2a = fX2 + fX2b
        fYa = fY + fYb

        def chain_f(b, i, t):
            pc = t % 2
            pl = t % 4
            an, bn, yn = f"fA{pc}", f"fB{pc}", f"fY{pl}"
            x2n = f"fX2{pl}"
            s4 = tiny_pool(f"F_s4{pc}")
            mu4 = tiny_pool(f"F_mu4{pc}")
            op("vector", [f"F_s4{pc}"], [f"F_mu4{pc}"],
               lambda h: h.tensor_scalar(mu4, s4, 1.0 / D, None, ALU.mult))
            nmu4 = tiny_pool(f"F_nmu4{pc}")
            op("vector", [f"F_s4{pc}"], [f"F_nmu4{pc}"],
               lambda h: h.tensor_scalar(nmu4, s4, -1.0 / D, None, ALU.mult))
            v4 = tiny_pool(f"F_v4{pc}")
            op("scalar", [yn, f"F_nmu4{pc}"], [an, f"F{pc}_acc"],
               lambda h, pc=pc, pl=pl: h.activation(fA[pc], fYa[pl][:, :],
                                                    AF.Square, bias=nmu4,
                                                    accum_out=v4))
            r4 = rstd(v4, D, EPS_OUT, f"F{pc}")
            nm4 = tiny_pool(f"F_nm4{pc}")
            op("vector", [f"F_mu4{pc}", f"F{pc}_r"], [f"F_nm4{pc}"],
               lambda h: h.tensor_scalar(nm4, mu4, r4, -1.0, ALU.mult,
                                         ALU.mult))
            op("scalar", [yn, f"F{pc}_r", f"F_nm4{pc}"], [bn],
               lambda h, pc=pc, pl=pl: h.activation(fB[pc], fYa[pl][:, :],
                                                    AF.Identity, scale=r4,
                                                    bias=nm4))
            on = f"fOU{pc}"
            op("vector", [bn, x2n], [on],
               lambda h, pc=pc, pl=pl: h.tensor_tensor(
                   fOU[pc][:, :], fB[pc], fX2a[pl], ALU.add))
            dma("scalar", dou[pc], [on], [],
                out_t[b, i * 128:(i + 1) * 128, :], fOU[pc])

        pend = []

        def emit_load(t):
            b, i, _ = items_f[t]
            pl = t % 4
            dma("sync", dx[pl], [], [f"fX2{pl}"], fX2a[pl],
                spill[b, i * 128:(i + 1) * 128, :])

        def emit_clip(t):
            pc, pq = t % 2, t % 4
            v, nm = xnf(t)
            an = f"fA{pc}"
            op("vector", [nm, "s_g"], [an],
               lambda h, v=v, pc=pc: h.tensor_scalar(
                   fA[pc], v, s_g, CLIP_HI, ALU.mult, ALU.min))
            op("vector", [an], [f"fXQF{pq}"],
               lambda h, pc=pc, pq=pq: h.tensor_scalar(
                   fXQFa[pq], fA[pc], CLIP_LO, None, ALU.max))

        def cxb_load(t):
            b, i, _ = items_f[t]
            sl = t % 4
            dma("gpsimd", dxb[sl], [], [f"cXB{sl}"], cXB[sl],
                xnsp[b, i * 128:(i + 1) * 128, :])

        def cxb_tr(t):
            sl, pc = t % 4, t % 2
            vlt = vLT0 if pc == 0 else vLT1
            psn = f"psS{pc}"

            def tr(h, vlt=vlt, sl=sl):
                for c in range(NCH):
                    last = h.transpose(vlt[:, c * 128:(c + 1) * 128],
                                       cXB[sl][:, c * 128:(c + 1) * 128],
                                       idn_bf[:, :])
                return last

            op("tensor", [f"cXB{sl}", "idn_bf"], [psn], tr)
            v, nm = xnf(t)
            op("vector", [psn], [nm],
               lambda h, v=v, vlt=vlt: h.tensor_copy(v, vlt))

        for u in range(NPRE, min(NPRE + 4, NIT)):
            cxb_load(u)
        emit_load(0)
        emit_load(1)
        emit_clip(0)
        emit_clip(1)
        for b, i, t in items_f:
            pc, pq = t % 2, t % 4

            if len(pend) >= 2:
                chain_f(*pend.pop(0))

            psY, pyn = (psA, "psA") if pc == 0 else (psB, "psB")

            def ymm(h, psY=psY, pq=pq):
                for uh in range(2):
                    for dc in range(NCH):
                        last = h.matmul(
                            psY[:, uh * 512:(uh + 1) * 512],
                            fXQFa[pq][:, dc * 128:(dc + 1) * 128],
                            WQS[:, dc * D + uh * 512:dc * D + (uh + 1) * 512],
                            start=(dc == 0), stop=(dc == NCH - 1))
                return last

            op("tensor", [f"fXQF{pq}", "WQS"], [pyn], ymm)
            if t + 2 < NIT:
                emit_load(t + 2)
                emit_clip(t + 2)
            if NPRE <= t + 4 < NIT:
                cxb_tr(t + 4)
            if NPRE + 4 <= t + 8 < NIT:
                cxb_load(t + 8)
            yn = f"fY{pq}"
            s4 = tiny_pool(f"F_s4{pc}")
            op("scalar", [pyn, "gb_g"], [yn, f"F_s4{pc}"],
               lambda h, psY=psY, pq=pq, s4=s4: h.activation(
                   fYa[pq], psY[:, :], AF.Identity, scale=gb_g,
                   accum_out=s4))
            pend.append((b, i, t))

        for it in pend:
            chain_f(*it)

        def fin(h):
            h.wait_ge(dou[0].sem, dou[0].n * 16)
            h.wait_ge(dou[1].sem, dou[1].n * 16)

        sec("scalar", fin)

    return nc


def kernel(**inputs):
    x = np.ascontiguousarray(np.asarray(inputs["L0_attn_x"], dtype=np.float32))
    args = {
        "w1": np.ascontiguousarray(np.asarray(inputs["L0_attn_W1"], np.float32)),
        "w2": np.ascontiguousarray(np.asarray(inputs["L0_attn_W2"], np.float32)),
        "w3": np.ascontiguousarray(np.asarray(inputs["L0_attn_W3"], np.float32)),
        "w4": np.ascontiguousarray(np.asarray(inputs["L0_attn_W4"], np.float32)),
        "wf": np.ascontiguousarray(np.asarray(inputs["L0_ffn_weight"], np.float32)),
        "b1": np.ascontiguousarray(np.asarray(inputs["L0_attn_b1"], np.float32)),
        "b2": np.ascontiguousarray(np.asarray(inputs["L0_attn_b2"], np.float32)),
        "b3": np.ascontiguousarray(np.asarray(inputs["L0_attn_b3"], np.float32)),
        "b4": np.ascontiguousarray(np.asarray(inputs["L0_attn_b4"], np.float32)),
        "gam": np.ascontiguousarray(np.asarray(inputs["L0_ffn_ln_gamma"], np.float32)),
        "bet": np.ascontiguousarray(np.asarray(inputs["L0_ffn_ln_beta"], np.float32)),
    }
    n_cores = 8
    nb = x.shape[0] // n_cores
    nc = build_nc(B=nb, T=x.shape[1])
    in_maps = []
    for c in range(n_cores):
        m = {"x": np.ascontiguousarray(x[c * nb:(c + 1) * nb])}
        m.update(args)
        in_maps.append(m)
    res = run_bass_kernel_spmd(nc, in_maps, core_ids=list(range(n_cores)))
    return np.concatenate([r["out"] for r in res.results], axis=0)


# version 36 (final)
# speedup vs baseline: 1.0379x; 1.0379x over previous
import numpy as np
import concourse.bass as bass
import concourse.mybir as mybir
from concourse.bass_utils import run_bass_kernel_spmd

F32 = mybir.dt.float32
BF16 = mybir.dt.bfloat16
U8 = mybir.dt.uint8
AF = mybir.ActivationFunctionType
ALU = mybir.AluOpType
AX = mybir.AxisListType

D = 1024
NCH = 8
QBF = 128.0
CLIP_HI = float(np.float32(QBF - 1e-5))
CLIP_LO = float(np.float32(-QBF + 1e-5))
ATT_SCALE = 0.03125
EPS_OUT = 1e-5
EPS_IN = 1e-3
EPS_W = 1e-5


class E:

    def __init__(self, sem):
        self.sem = sem
        self.n = 0
        self.seen = {}
        self.h = None

    def w(self, tok):
        if tok is None:
            return
        sem, val = tok
        if self.seen.get(sem.name, 0) >= val:
            return
        self.h.wait_ge(sem, val)
        self.seen[sem.name] = val

    def wmany(self, toks):
        for t in toks:
            self.w(t)

    def d(self, inst):
        inst.then_inc(self.sem, 1)
        self.n += 1
        return (self.sem, self.n)

    def t(self):
        return (self.sem, self.n)


class Buf:

    def __init__(self, name):
        self.name = name
        self.lw = None
        self.lr = []

    def rd_waits(self):
        return [self.lw]

    def wr_waits(self):
        return ([self.lw] if self.lw else []) + list(self.lr)

    def note_rd(self, tok):
        self.lr.append(tok)

    def note_wr(self, tok):
        self.lw = tok
        self.lr = []


def build_nc(B=2, T=2048, debug=False):
    NT = T // 128
    NSL = T // 512
    nc = bass.Bass("TRN2", target_bir_lowering=False)

    x_in = nc.dram_tensor("x", [B, T, D], F32, kind="ExternalInput").ap()
    out_t = nc.dram_tensor("out", [B, T, D], F32, kind="ExternalOutput").ap()
    w_hbm = {}
    for nm in ("w1", "w2", "w3", "w4", "wf"):
        w_hbm[nm] = nc.dram_tensor(nm, [D, D], F32, kind="ExternalInput").ap()
    vecs = {}
    for nm in ("b1", "b2", "b3", "b4", "gam", "bet"):
        vecs[nm] = nc.dram_tensor(nm, [D], F32, kind="ExternalInput").ap()
    spill = nc.dram_tensor("spill", [B, T, D], F32).ap()
    xnsp = nc.dram_tensor("xnsp", [B, T, D], BF16).ap()
    sc128 = nc.dram_tensor("sc128", [128], F32).ap()
    sc128b = nc.dram_tensor("sc128b", [128], F32).ap()
    sc1a = nc.dram_tensor("sc1a", [1, 1], F32).ap()
    lrowb = nc.dram_tensor("lrowb_d", [1, 1], F32).ap()
    cc_in = nc.dram_tensor("cc_in", [1, 1], F32).ap()
    cc_out = nc.dram_tensor("cc_out", [1, 1], F32).ap()

    def sb(name, shape, dt):
        return nc.alloc_sbuf_tensor(name, shape, dt).ap()

    QF = sb("QF", [128, NCH * T], BF16)
    KF = sb("KF", [128, NCH * T], BF16)
    VT = sb("VT", [128, NT * D], BF16)
    W4S = sb("W4S", [128, NCH * D], BF16)
    WQS = sb("WQS", [128, NCH * D], BF16)
    idn_bf = sb("idn_bf", [128, 128], BF16)
    b1c = sb("b1c", [128, NCH], F32)
    b2c = sb("b2c", [128, NCH], F32)
    lrow = sb("lrow", [1, 128], F32)
    l11 = sb("l11", [1, 1], F32)
    lrowB = sb("lrowB", [1, 128], F32)
    l11b = sb("l11b", [1, 1], F32)
    TINY = sb("TINY", [128, 64], F32)
    tiny_i = [0]

    def tiny():
        c = tiny_i[0]
        tiny_i[0] += 1
        assert c < 64
        return TINY[:, c:c + 1]

    rbeta_bc = tiny()
    betaw_bc = tiny()
    gam_g = tiny()
    s_g = tiny()
    gb_g = tiny()
    gmax = [tiny(), tiny()]

    ARENA_BYTES = 75776
    arena = sb("arena", [128, ARENA_BYTES], U8)

    def ar(off, nelem, dt, width):
        return arena[:, off:off + nelem * width].bitcast(dt)

    def arf(off, n):
        return ar(off, n, F32, 4)

    def arb(off, n):
        return ar(off, n, BF16, 2)

    pS0 = arf(0, 4096)
    pS1 = arf(16384, 4096)
    pTH = arf(32768, 4096)
    pRH = arf(49152, 4096)
    pIOTA = arf(73728, 128)
    idn_f = arf(74240 // 4 * 0 + 74240, 128)

    LN1F = arb(0, 16384)
    pXT = [arf(32768, 1024), arf(36864, 1024), arf(40960, 1024)]
    pSCR = [arf(45056, 1024)]
    pLNT = [arb(49152, 1024), arb(51200, 1024)]
    pB3 = arf(53248, 1024)
    WBUF = arb(57344, 8192)

    lP = arb(0, 2048)
    lPT = arb(4096, 2048)
    lOT = arb(8192, 1024)
    lOF = arb(10240, 1024)
    lZ = arf(12288, 1024)
    lXT = [arf(16384, 1024), arf(20480, 1024)]
    lX2 = [arf(24576, 1024), arf(28672, 1024)]
    lA = [arf(32768, 1024), arf(36864, 1024)]
    lBu = [arf(40960, 1024), arf(45056, 1024)]
    lB4 = arf(49152, 1024)
    lGAMb = arb(53248, 1024)
    lBETb = arb(55296, 1024)
    lW16 = [arb(40960, 1024), arb(45056, 1024)]
    lXN1 = [arb(32768, 1024), arb(36864, 1024)]
    lXN2 = [arb(40960 + 2048, 1024), arb(45056 + 2048, 1024)]

    fX2 = [arf(0, 1024), arf(4096, 1024)]
    cXB = [arb(8192, 1024), arb(10240, 1024),
           arb(61440, 1024), arb(63488, 1024)]
    fXQF = [arb(16384, 1024), arb(18432, 1024)]
    fY = [arf(20480, 1024), arf(24576, 1024)]
    fOU = [arf(28672, 1024), arf(32768, 1024)]
    fA = [arf(36864, 1024), arf(40960, 1024)]
    fB = [arf(45056, 1024), arf(49152, 1024)]
    fX2b = [arf(53248, 1024), arf(57344, 1024)]
    fYb = [arf(65536, 1024), arf(69632, 1024)]
    cXN = [arb(65536, 1024), arb(67584, 1024),
           arb(69632, 1024), arb(71680, 1024)]
    fXQFa = [arb(16384, 1024), arb(18432, 1024),
             arb(12288, 1024), arb(14336, 1024)]

    psS = nc.psum_tensor("psS", [128, 2048], F32).__enter__()
    psA = nc.psum_tensor("psA", [128, 1024], F32).__enter__()
    psB = nc.psum_tensor("psB", [128, 1024], F32).__enter__()
    psS0 = psS[:, 0:1024]
    psS1 = psS[:, 1024:2048]
    vLT0 = psS[:, 0:512].bitcast(BF16)
    vLT1 = psS[:, 1024:1536].bitcast(BF16)
    vPT = psA[:, :].bitcast(BF16)
    vOF = psB[:, 0:512].bitcast(BF16)

    def sem(n):
        return nc.semaphore(n).__enter__()

    PE = E(sem("s_pe"))
    VE = E(sem("s_ve"))
    SE = E(sem("s_se"))
    GE = E(sem("s_ge"))
    SY = E(sem("s_sy"))
    ENGS = {"tensor": PE, "vector": VE, "scalar": SE, "gpsimd": GE, "sync": SY}

    class DS:
        def __init__(self, name):
            self.sem = sem(name)
            self.n = 0

        def go(self, q, out, in_, **kw):
            q.h.dma_start(out=out, in_=in_, **kw).then_inc(self.sem, 16)
            self.n += 1
            return (self.sem, self.n * 16)

    dx = [DS("d_x0"), DS("d_x1"), DS("d_x2"), DS("d_x3")]
    dxn = [DS("d_xn0"), DS("d_xn1"), DS("d_xn2"), DS("d_xn3")]
    dw = {nm: DS("d_" + nm) for nm in ("w1", "w2", "w3")}
    dwc = [DS(f"d_wc{c}") for c in range(NCH)]
    dwf = [DS(f"d_wf{c}") for c in range(NCH)]
    dxb = [DS(f"d_xb{c}") for c in range(4)]
    dgm = DS("d_gm")
    dpm = DS("d_pm")
    dbc = DS("d_bc")
    dsp = [DS("d_sp0"), DS("d_sp1")]
    dxs = [DS("d_xs0"), DS("d_xs1")]
    dou = [DS("d_ou0"), DS("d_ou1")]
    dcc = DS("d_cc")
    dcc2 = DS("d_cc2")
    ccs = sem("s_cc")

    bufs = {}

    def bget(name):
        if name not in bufs:
            bufs[name] = Buf(name)
        return bufs[name]

    cur_blk = [None]

    def sec(ename, body):
        eng = ENGS[ename]
        getattr(cur_blk[0], ename)(lambda h: (setattr(eng, "h", h), body(h))[0])

    def op(ename, reads, writes, emit):
        eng = ENGS[ename]
        tok_out = [None]

        def body(h):
            for r in reads:
                eng.wmany(bget(r).rd_waits())
            for wn in writes:
                eng.wmany(bget(wn).wr_waits())
            tok = eng.d(emit(h))
            for r in reads:
                bget(r).note_rd(tok)
            for wn in writes:
                bget(wn).note_wr(tok)
            tok_out[0] = tok

        sec(ename, body)
        return tok_out[0]

    def dma(qname, stream, reads, writes, out, in_, **kw):
        eng = ENGS[qname]
        tok_out = [None]

        def body(h):
            for r in reads:
                eng.wmany(bget(r).rd_waits())
            for wn in writes:
                eng.wmany(bget(wn).wr_waits())
            tok = stream.go(eng, out, in_, **kw)
            for r in reads:
                bget(r).note_rd(tok)
            for wn in writes:
                bget(wn).note_wr(tok)
            tok_out[0] = tok

        sec(qname, body)
        return tok_out[0]

    def dma_batch(qname, stream, entries):
        eng = ENGS[qname]

        def body(h):
            for reads, writes, out, in_, kw in entries:
                for r in reads:
                    eng.wmany(bget(r).rd_waits())
                for wn in writes:
                    eng.wmany(bget(wn).wr_waits())
                stream.go(eng, out, in_, **kw)
            tok = (stream.sem, stream.n * 16)
            for reads, writes, out, in_, kw in entries:
                for r in reads:
                    bget(r).note_rd(tok)
                for wn in writes:
                    bget(wn).note_wr(tok)

        sec(qname, body)

    eps_ap = {}

    def rstd(vsum, n, eps, tag):
        lt = tiny_pool(tag + "_lt")
        r = tiny_pool(tag + "_r")
        op("scalar", [tag + "_acc", "eps_c"], [tag + "_lt"],
           lambda h: h.activation(lt, vsum, AF.Ln, scale=1.0 / n,
                                  bias=eps_ap[eps]))
        op("scalar", [tag + "_lt"], [tag + "_r"],
           lambda h: h.activation(r, lt, AF.Exp, scale=-0.5))
        return r

    from concourse.hw_specs import get_activation_tables
    _act_set = list(get_activation_tables(nc.m.arch)).index(
        "natural_log_exp_and_others")

    def preload_tables():
        def body(h):
            h.add_instruction(mybir.InstLoadActFuncSet(
                name=nc.get_next_instruction_name(),
                act_func_set_id=_act_set, ins=[], outs=[]))

        sec("scalar", body)

    tiny_slots = {}

    def tiny_pool(tag):
        if tag not in tiny_slots:
            tiny_slots[tag] = tiny()
        return tiny_slots[tag]

    ones_like = None

    def wchunk_load(wname):
        for c in range(NCH):
            dma("gpsimd", dwc[c], [f"WB{c}"], [f"WB{c}"],
                WBUF[:, c * 1024:(c + 1) * 1024],
                w_hbm[wname][c * 128:(c + 1) * 128, :])

    def emit_prologue():

            entries = [
                ([], ["W4S"], W4S.rearrange("p (c u) -> p c u", c=NCH),
                 w_hbm["w4"].rearrange("(c p) u -> p c u", p=128), {}),
            ]
            for c in range(NCH):
                entries.append(([], ["b1c"], b1c[:, c:c + 1],
                                vecs["b1"][c * 128:(c + 1) * 128].rearrange(
                                    "(p o) -> p o", o=1), {}))
                entries.append(([], ["b2c"], b2c[:, c:c + 1],
                                vecs["b2"][c * 128:(c + 1) * 128].rearrange(
                                    "(p o) -> p o", o=1), {}))
            dma_batch("gpsimd", dpm, entries)
            for v in (EPS_OUT, EPS_IN):
                eps_ap[v] = tiny()
                op("vector", [], ["eps_c"],
                   lambda h, v=v: h.memset(eps_ap[v], v))
            op("gpsimd", [], ["iota"],
               lambda h: h.iota(pIOTA, [[1, 128]], base=0, channel_multiplier=-1,
                                allow_small_or_imprecise_dtypes=True))
            op("vector", ["iota"], ["idn_bf"],
               lambda h: h.tensor_scalar(idn_bf[:, :], pIOTA, 0.0, None, ALU.is_equal))
            op("vector", ["iota"], ["idn_f"],
               lambda h: h.tensor_scalar(idn_f[:, :], pIOTA, 0.0, None, ALU.is_equal))



    def phase_P(b, w1_prefetched=False):
        with nc.Block() as blk:
            cur_blk[0] = blk
            preload_tables()
            if b == 0:
                emit_prologue()
            dma_batch("gpsimd", dbc, [([], ["pB3"], pB3,
                       vecs["b3"].unsqueeze(0).partition_broadcast(128), {})])
            if not w1_prefetched:
                wchunk_load("w1")

            slots = [(psA, 0, "psA0"), (psA, 512, "psA1"),
                     (psB, 0, "psB0"), (psB, 512, "psB1")]

            def emit_tile(i):
                p = i % 2
                px = i % 3
                xb, sc = f"pXT{px}", "pSCR0"
                dma("sync", dx[px], [], [xb], pXT[px],
                    x_in[b, i * 128:(i + 1) * 128, :])
                s1 = tiny_pool("P_s1")
                op("scalar", [xb], [sc, "P_s1"],
                   lambda h: h.activation(pSCR[0], pXT[px], AF.Identity,
                                          accum_out=s1))
                mu1 = tiny_pool("P_mu1")
                op("vector", ["P_s1"], ["P_mu1"],
                   lambda h: h.tensor_scalar(mu1, s1, 1.0 / D, None, ALU.mult))
                nm1 = tiny_pool("P_nm1")
                op("vector", ["P_s1"], ["P_nm1"],
                   lambda h: h.tensor_scalar(nm1, s1, -1.0 / D, None, ALU.mult))
                v1 = tiny_pool("P_v1")
                op("scalar", [xb, "P_nm1"], [sc, "P_acc"],
                   lambda h: h.activation(pSCR[0], pXT[px], AF.Square, bias=nm1,
                                          accum_out=v1))
                r1 = rstd(v1, D, EPS_OUT, "P")
                lt = f"pLNT{p}"
                op("vector", [xb, "P_mu1", "P_r"], [lt],
                   lambda h: h.tensor_scalar(pLNT[p], pXT[px], mu1, r1,
                                             ALU.subtract, ALU.mult))
                vlt = vLT0 if p == 0 else vLT1
                psn = f"psS{p}"

                def tr(h, vlt=vlt, p=p):
                    for c in range(NCH):
                        last = h.transpose(vlt[:, c * 128:(c + 1) * 128],
                                           pLNT[p][:, c * 128:(c + 1) * 128],
                                           idn_bf[:, :])
                    return last

                op("tensor", [lt, "idn_bf"], [psn], tr)
                op("vector", [psn], ["LN1F"],
                   lambda h, vlt=vlt, i=i: h.tensor_copy(
                       LN1F[:, 0:NCH * T].rearrange(
                           "p (c t) -> p c t", c=NCH)[
                           :, :, i * 128:(i + 1) * 128],
                       vlt.rearrange("p (c t) -> p c t", c=NCH)))

            def proj_pass(groups, kind, dst_ap=None, dst_name=None,
                          bias_c=None, bias_name=None, split=False):
                gs = [(g, slots[j]) for j, g in enumerate(groups)]

                def one(h, g, ps, of, dc):
                    if kind == "fm":
                        k, ec = g
                        return h.matmul(
                            ps[:, of:of + 512],
                            WBUF[:, dc * 1024 + ec * 128:dc * 1024 + (ec + 1) * 128],
                            LN1F[:, dc * T + k * 512:dc * T + (k + 1) * 512],
                            start=(dc == 0), stop=(dc == NCH - 1))
                    tc, uh = g
                    return h.matmul(
                        ps[:, of:of + 512],
                        LN1F[:, dc * T + tc * 128:dc * T + (tc + 1) * 128],
                        WBUF[:, dc * 1024 + uh * 512:dc * 1024 + (uh + 1) * 512],
                        start=(dc == 0), stop=(dc == NCH - 1))

                if split:
                    for dc in range(NCH):
                        def mmc(h, dc=dc):
                            for g, (ps, of, pn) in gs:
                                last = one(h, g, ps, of, dc)
                            return last

                        op("tensor", [f"WB{dc}", "LN1F"],
                           [pn for _, (_, _, pn) in gs], mmc)
                else:
                    for g, (ps, of, pn) in gs:
                        def mm(h, g=g, ps=ps, of=of):
                            for dc in range(NCH):
                                last = one(h, g, ps, of, dc)
                            return last

                        op("tensor", [f"WB{c}" for c in range(NCH)] + ["LN1F"],
                           [pn], mm)
                for g, (ps, of, pn) in gs:
                    if kind == "fm":
                        k, ec = g
                        op("scalar", [pn, bias_name], [dst_name],
                           lambda h, ps=ps, of=of, ec=ec, k=k:
                           h.activation(
                               dst_ap[:, ec * T + k * 512:ec * T + (k + 1) * 512],
                               ps[:, of:of + 512], AF.Identity,
                               bias=bias_c[:, ec:ec + 1]))
                    else:
                        tc, uh = g
                        op("vector", [pn, "pB3"], ["VT"],
                           lambda h, ps=ps, of=of, tc=tc, uh=uh:
                           h.tensor_tensor(
                               VT[:, tc * D + uh * 512:tc * D + (uh + 1) * 512],
                               ps[:, of:of + 512],
                               pB3[:, uh * 512:(uh + 1) * 512], ALU.add))

            qgroups = [(k, ec) for k in range(NSL) for ec in range(NCH)]
            nq = len(qgroups)
            ntile_done = 0
            for j in range(0, nq, 4):
                need_k = max(k for k, _ in qgroups[j:j + 4])
                while ntile_done < min((need_k + 1) * 4, NT):
                    emit_tile(ntile_done)
                    ntile_done += 1
                proj_pass(qgroups[j:j + 4], "fm", QF, "QF", b1c, "b1c",
                          split=(j + 4 >= nq))
            while ntile_done < NT:
                emit_tile(ntile_done)
                ntile_done += 1
            wchunk_load("w2")
            for j in range(0, nq, 4):
                proj_pass(qgroups[j:j + 4], "fm", KF, "KF", b2c, "b2c",
                          split=(j == 0 or j + 4 >= nq))
            wchunk_load("w3")
            vgroups = [(tc, uh) for tc in range(NT) for uh in range(2)]
            for j in range(0, len(vgroups), 4):
                proj_pass(vgroups[j:j + 4], "v", split=(j == 0))

    gm_tick = [0]

    def phase_L(b):
        with nc.Block() as blk:
            cur_blk[0] = blk
            preload_tables()
            dma_batch("gpsimd", dbc, [
                ([], ["lB4"], lB4,
                 vecs["b4"].unsqueeze(0).partition_broadcast(128), {}),
                ([], ["lGAM"], lGAMb,
                 vecs["gam"].unsqueeze(0).partition_broadcast(128), {}),
                ([], ["lBET"], lBETb,
                 vecs["bet"].unsqueeze(0).partition_broadcast(128), {}),
            ])

            NKH = max(1, T // 1024)
            KW = T // NKH

            def chain_rest(q):
                p = q % 2
                A, Bu = lA[p], lBu[p]
                an, bn = f"lA{p}", f"lBu{p}"
                xn_, x2n = f"lXT{p}", f"lX2{p}"
                sz = tiny_pool("L_sz")
                muz = tiny_pool("L_muz")
                op("vector", ["L_sz"], ["L_muz"],
                   lambda h: h.tensor_scalar(muz, sz, 1.0 / D, None, ALU.mult))
                nmz = tiny_pool("L_nmz")
                op("vector", ["L_sz"], ["L_nmz"],
                   lambda h: h.tensor_scalar(nmz, sz, -1.0 / D, None, ALU.mult))
                vz = tiny_pool("L_vz")
                op("scalar", ["lZ", "L_nmz"], [an, "L_acc"],
                   lambda h: h.activation(A, lZ[:, :], AF.Square, bias=nmz,
                                          accum_out=vz))
                rz = rstd(vz, D, EPS_OUT, "L")
                op("vector", ["lZ", "L_muz", "L_r"], [an],
                   lambda h: h.tensor_scalar(A, lZ[:, :], muz, rz,
                                             ALU.subtract, ALU.mult))
                s2 = tiny_pool("L_s2")
                op("vector", [an, xn_], [x2n, "L_s2"],
                   lambda h: h.scalar_tensor_tensor(
                       lX2[p][:, :], A, 1.0, lXT[p], ALU.bypass, ALU.add,
                       accum_out=s2))
                dma("scalar", dsp[p], [x2n], [],
                    spill[b, q * 128:(q + 1) * 128, :], lX2[p])
                mu2 = tiny_pool("L_mu2")
                op("vector", ["L_s2"], ["L_mu2"],
                   lambda h: h.tensor_scalar(mu2, s2, 1.0 / D, None, ALU.mult))
                nm2 = tiny_pool("L_nm2")
                op("vector", ["L_s2"], ["L_nm2"],
                   lambda h: h.tensor_scalar(nm2, s2, -1.0 / D, None, ALU.mult))
                v2 = tiny_pool("L_v2")
                op("scalar", [x2n, "L_nm2"], [an, "L2_acc"],
                   lambda h: h.activation(A, lX2[p][:, :], AF.Square, bias=nm2,
                                          accum_out=v2))
                r2 = rstd2(v2, D, EPS_OUT, "L2")
                op("vector", [x2n, "L_mu2", "L2_r"], [an],
                   lambda h: h.tensor_scalar(A, lX2[p][:, :], mu2, r2,
                                             ALU.subtract, ALU.mult))
                d3 = tiny_pool("L_d3")
                op("vector", ["L_s2", "L_mu2"], ["L_d3"],
                   lambda h: h.tensor_scalar(d3, s2, 1.0 / D, mu2,
                                             ALU.mult, ALU.subtract))
                mu3 = tiny_pool("L_mu3")
                op("vector", ["L_d3", "L2_r"], ["L_mu3"],
                   lambda h: h.tensor_scalar(mu3, d3, r2, None, ALU.mult))
                nm3 = tiny_pool("L_nm3")
                op("vector", ["L_mu3"], ["L_nm3"],
                   lambda h: h.tensor_scalar(nm3, mu3, -1.0, None, ALU.mult))
                v3 = tiny_pool("L_v3")
                op("scalar", [an, "L_nm3"], [bn, "L3_acc"],
                   lambda h: h.activation(Bu, A, AF.Square, bias=nm3,
                                          accum_out=v3))
                r3 = rstd2(v3, D, EPS_IN, "L3")
                op("vector", [an, "L_mu3", "L3_r"], [bn],
                   lambda h: h.tensor_scalar(lW16[p], A, mu3, r3,
                                             ALU.subtract, ALU.mult))
                op("vector", [bn, "lGAM"], [an],
                   lambda h: h.tensor_tensor(lXN1[p], lW16[p], lGAMb, ALU.mult))
                op("vector", [an, "lBET"], [bn],
                   lambda h: h.tensor_tensor(lXN2[p], lXN1[p], lBETb, ALU.add))
                mxq = tiny_pool("L_mxq")
                op("vector", [bn], ["L_mxq"],
                   lambda h: h.tensor_reduce(mxq, lXN2[p], AX.X, ALU.max,
                                             apply_absolute_value=True))
                gi = gm_tick[0]
                gm_tick[0] += 1
                if gi == 0:
                    op("vector", ["L_mxq"], ["gmax"],
                       lambda h: h.tensor_copy(gmax[0], mxq))
                else:
                    op("vector", ["L_mxq", "gmax"], ["gmax"],
                       lambda h, gi=gi: h.tensor_tensor(
                           gmax[gi % 2], gmax[(gi - 1) % 2], mxq, ALU.max))
                dma("gpsimd", dxs[p], [bn], [],
                    xnsp[b, q * 128:(q + 1) * 128, :], lXN2[p])

            def emit_ofz(q):

                def tro(h):
                    for c in range(NCH):
                        last = h.transpose(vOF[:, c * 128:(c + 1) * 128],
                                           lOT[:, c * 128:(c + 1) * 128],
                                           idn_bf[:, :])
                    return last

                op("tensor", ["lOT", "idn_bf"], ["psB"], tro)
                op("scalar", ["psB"], ["lOF"],
                   lambda h: h.activation(lOF[:, :], vOF, AF.Identity))

                def zmm(h):
                    for uh in range(2):
                        for ec in range(NCH):
                            last = h.matmul(
                                psB[:, uh * 512:(uh + 1) * 512],
                                lOF[:, ec * 128:(ec + 1) * 128],
                                W4S[:, ec * D + uh * 512:ec * D + (uh + 1) * 512],
                                start=(ec == 0), stop=(ec == NCH - 1))
                    return last

                op("tensor", ["lOF", "W4S"], ["psB"], zmm)
                sz = tiny_pool("L_sz")
                op("vector", ["psB", "lB4"], ["lZ", "L_sz"],
                   lambda h: h.scalar_tensor_tensor(
                       lZ[:, :], psB[:, :], 1.0, lB4, ALU.bypass, ALU.add,
                       accum_out=sz))

            def emit_scores(q):
                for half in range(NKH):
                    psn = f"psS{half}"

                    def mm(h, half=half, q=q):
                        for t2 in range(KW // 512):
                            tq = half * (KW // 512) + t2
                            for ec in range(NCH):
                                last = h.matmul(
                                    psS[:, tq * 512:(tq + 1) * 512],
                                    QF[:, ec * T + q * 128:ec * T + (q + 1) * 128],
                                    KF[:, ec * T + tq * 512:ec * T + (tq + 1) * 512],
                                    start=(ec == 0), stop=(ec == NCH - 1))
                        return last

                    op("tensor", ["QF", "KF"], [psn], mm)

            emit_scores(0)
            for q in range(NT):
                p = q % 2
                dma("sync", dx[p], [], [f"lXT{p}"], lXT[p],
                    x_in[b, q * 128:(q + 1) * 128, :])
                ng0 = tiny_pool("L_ng0")
                op("vector", ["psS0", "psS1"], ["L_ng0"],
                   lambda h: h.reduce_max(ng0, psS[:, 0:T], axis=AX.X,
                                          negate=True))
                ngs = tiny_pool("L_ngs")
                op("vector", ["L_ng0"], ["L_ngs"],
                   lambda h: h.tensor_scalar(ngs, ng0, ATT_SCALE, None,
                                             ALU.mult))

                if q > 0:
                    emit_ofz(q - 1)

                se0 = tiny_pool("L_se0")
                op("scalar", ["psS0", "psS1", "L_ngs"], ["lP", "L_se0"],
                   lambda h: h.activation(lP[:, 0:T], psS[:, 0:T], AF.Exp,
                                          scale=ATT_SCALE, bias=ngs,
                                          accum_out=se0))
                rs = tiny_pool("L_rs")
                op("vector", ["L_se0"], ["L_rs"],
                   lambda h: h.reciprocal(rs, se0))

                def trp(h):
                    for j in range(NT):
                        last = h.transpose(vPT[:, j * 128:(j + 1) * 128],
                                           lP[:, j * 128:(j + 1) * 128],
                                           idn_bf[:, :])
                    return last

                op("tensor", ["lP", "idn_bf"], ["psA"], trp)
                op("scalar", ["psA"], ["lPT"],
                   lambda h: h.activation(lPT[:, 0:NT * 128],
                                          vPT[:, 0:NT * 128], AF.Identity))

                if q + 1 < NT:
                    emit_scores(q + 1)

                if q > 0:
                    chain_rest(q - 1)


                def pv(h):
                    for uh in range(2):
                        for tc in range(NT):
                            last = h.matmul(
                                psA[:, uh * 512:(uh + 1) * 512],
                                lPT[:, tc * 128:(tc + 1) * 128],
                                VT[:, tc * D + uh * 512:tc * D + (uh + 1) * 512],
                                start=(tc == 0), stop=(tc == NT - 1))
                    return last

                op("tensor", ["lPT", "VT"], ["psA"], pv)
                op("scalar", ["psA", "L_rs"], ["lOT"],
                   lambda h: h.activation(lOT[:, :], psA[:, :], AF.Identity,
                                          scale=rs))

            emit_ofz(NT - 1)
            chain_rest(NT - 1)

            if b + 1 < B:
                wchunk_load("w1")
            else:
                gfin = gmax[(gm_tick[0] - 1) % 2]
                op("tensor", ["gmax", "idn_f"], ["psS0"],
                   lambda h: h.transpose(psS[0:1, 512:640], gfin,
                                         idn_f[:, :]))
                op("vector", ["psS0"], ["l11"],
                   lambda h: h.tensor_reduce(l11, psS[0:1, 512:640], AX.X,
                                             ALU.max))
                dma("sync", dgm, ["l11"], ["cc_in"], cc_in, l11)

                def coll_issue(h):
                    GE.wmany(bget("cc_in").rd_waits())
                    h.collective_compute(
                        "AllReduce", ALU.max, replica_groups=[list(range(8))],
                        ins=[cc_in.opt()], outs=[cc_out.opt()]).then_inc(ccs, 1)

                sec("gpsimd", coll_issue)

            def drain_sp(h):
                h.wait_ge(dsp[0].sem, dsp[0].n * 16)
                h.wait_ge(dsp[1].sem, dsp[1].n * 16)

            sec("scalar", drain_sp)

            def drain_xs(h):
                h.wait_ge(dxs[0].sem, dxs[0].n * 16)
                h.wait_ge(dxs[1].sem, dxs[1].n * 16)

            sec("gpsimd", drain_xs)

    def rstd2(vsum, n, eps, tag):
        return rstd(vsum, n, eps, tag)

    if debug:
        d_ln = nc.dram_tensor("d_ln", [128, NCH * T], BF16,
                              kind="ExternalOutput").ap()
        d_qf = nc.dram_tensor("d_qf", [128, NCH * T], BF16,
                              kind="ExternalOutput").ap()
        d_vt = nc.dram_tensor("d_vt", [128, NT * D], BF16,
                              kind="ExternalOutput").ap()
    ddma = DS("d_dbg")

    def dump_dbg():
        with nc.Block() as blk:
            cur_blk[0] = blk

            def body(h):
                h.dma_start(out=d_ln, in_=LN1F[:, 0:NCH * T]).then_inc(
                    ddma.sem, 16)
                h.dma_start(out=d_qf, in_=QF).then_inc(ddma.sem, 16)
                h.dma_start(out=d_vt, in_=VT).then_inc(ddma.sem, 16)
                h.wait_ge(ddma.sem, 48)

            sec("sync", body)

    for b in range(B):
        phase_P(b, w1_prefetched=(b > 0))
        if debug and b == 0:
            dump_dbg()
        phase_L(b)

    items_f = [(b, i, b * NT + i) for b in range(B) for i in range(NT)]
    NIT = len(items_f)
    NPRE = 8 if NIT >= 16 else max(1, NIT // 2)

    def xnf(t):
        if t < NT:
            return QF[:, t * 1024:(t + 1) * 1024], "QF"
        return KF[:, (t - NT) * 1024:(t - NT + 1) * 1024], "KF"

    with nc.Block() as blk:
        cur_blk[0] = blk
        preload_tables()

        def cxn_load(t):
            b, i, _ = items_f[t]
            sl = t % 4
            dma("sync" if sl % 2 == 0 else "scalar", dxn[sl], [],
                [f"cXN{sl}"], cXN[sl], xnsp[b, i * 128:(i + 1) * 128, :])

        def cxn_tr(t):
            sl, pc = t % 4, t % 2
            vlt = vLT0 if pc == 0 else vLT1
            psn = f"psS{pc}"

            def tr(h, vlt=vlt, sl=sl):
                for c in range(NCH):
                    last = h.transpose(vlt[:, c * 128:(c + 1) * 128],
                                       cXN[sl][:, c * 128:(c + 1) * 128],
                                       idn_bf[:, :])
                return last

            op("tensor", [f"cXN{sl}", "idn_bf"], [psn], tr)
            v, nm = xnf(t)
            if t % 2 == 0:
                op("scalar", [psn], [nm],
                   lambda h, v=v, vlt=vlt: h.activation(v, vlt, AF.Identity))
            else:
                op("vector", [psn], [nm],
                   lambda h, v=v, vlt=vlt: h.tensor_copy(v, vlt))

        for u in range(min(4, NPRE)):
            cxn_load(u)
        pS0c = pS0.rearrange("p (c u) -> p c u", c=4)
        pS1c = pS1.rearrange("p (c u) -> p c u", c=4)
        for c in range(4):
            dma("sync", dwf[c], [], [f"pS0c{c}"], pS0c[:, c, :],
                w_hbm["wf"][c * 128:(c + 1) * 128, :])
            dma("scalar", dwf[4 + c], [], [f"pS1c{c}"], pS1c[:, c, :],
                w_hbm["wf"][512 + c * 128:512 + (c + 1) * 128, :])

        for t in range(NPRE):
            cxn_tr(t)
            if t + 4 < NPRE:
                cxn_load(t + 4)
        asum = tiny_pool("asum")
        op("vector", [f"pS0c{c}" for c in range(4)], ["asum0"],
           lambda h: h.tensor_reduce(asum, pS0, AX.X, ALU.add,
                                     apply_absolute_value=True))
        a2t = tiny_pool("asum2")
        op("vector", [f"pS1c{c}" for c in range(4)], ["asum1"],
           lambda h: h.tensor_reduce(a2t, pS1, AX.X, ALU.add,
                                     apply_absolute_value=True))
        op("vector", ["asum0", "asum1"], ["asum"],
           lambda h: h.tensor_tensor(asum, asum, a2t, ALU.add))
        dma("gpsimd", dcc2, ["asum"], ["sc128b"],
            sc128b.rearrange("(p o) -> p o", o=1), asum)
        dma("gpsimd", dcc2, ["sc128b"], ["lrowB"], lrowB,
            sc128b.rearrange("(a b) -> a b", a=1))
        op("vector", ["lrowB"], ["l11b"],
           lambda h: h.tensor_reduce(l11b, lrowB, AX.X, ALU.add))
        dma("gpsimd", dcc2, ["l11b"], ["sc1a"], sc1a, l11b)
        dma("gpsimd", dcc2, ["sc1a"], ["betaw"], betaw_bc,
            sc1a.partition_broadcast(128))
        op("vector", ["betaw"], ["betaw"],
           lambda h: h.tensor_scalar(betaw_bc, betaw_bc, 1.0 / (D * D), None,
                                     ALU.mult))
        rb_t = tiny_pool("rb_t")
        op("vector", ["betaw"], ["rb_t"],
           lambda h: h.tensor_scalar(rb_t, betaw_bc, EPS_W, None, ALU.add))
        op("vector", ["rb_t"], ["rbeta"],
           lambda h: h.reciprocal(rbeta_bc, rb_t))

        for hn, S_h in (("h0", pS0), ("h1", pS1)):
            chn = [f"pS0c{c}" if hn == "h0" else f"pS1c{c}" for c in range(4)]
            op("vector", chn + ["rbeta"], ["pTH"],
               lambda h, S_h=S_h: h.tensor_scalar(pTH, S_h, rbeta_bc, None,
                                                  ALU.mult))
            op("vector", ["pTH"], ["pS0" if hn == "h0" else "pS1"],
               lambda h, S_h=S_h: h.tensor_scalar(S_h, pTH, 0.5, None,
                                                  ALU.is_gt))
            op("vector", ["pTH"], ["pRH"],
               lambda h: h.tensor_scalar(pRH, pTH, -0.5, None, ALU.is_lt))
            off = 0 if hn == "h0" else 4096
            op("vector", ["pS0" if hn == "h0" else "pS1", "pRH"], ["WQS"],
               lambda h, S_h=S_h, off=off: h.tensor_tensor(
                   WQS[:, off:off + 4096], S_h, pRH, ALU.subtract))

        def coll_wait(h):
            h.wait_ge(ccs, 1)

        sec("sync", coll_wait)
        dma("sync", dgm, [], ["gam_g"], gam_g,
            cc_out.partition_broadcast(128))
        tg = tiny_pool("C_tg")
        op("vector", ["gam_g"], ["C_tg"],
           lambda h: h.reciprocal(tg, gam_g))
        op("vector", ["C_tg"], ["s_g"],
           lambda h: h.tensor_scalar(s_g, tg, QBF, None, ALU.mult))
        op("vector", ["gam_g", "betaw"], ["gb_g"],
           lambda h: h.tensor_scalar(gb_g, gam_g, betaw_bc, 1.0 / QBF,
                                     ALU.mult, ALU.mult))

    with nc.Block() as blk:
        cur_blk[0] = blk
        preload_tables()

        def wsp(h):
            h.wait_ge(dsp[0].sem, dsp[0].n * 16)
            h.wait_ge(dsp[1].sem, dsp[1].n * 16)
            h.wait_ge(dxs[0].sem, dxs[0].n * 16)
            h.wait_ge(dxs[1].sem, dxs[1].n * 16)

        sec("sync", wsp)

        tick = [0]
        fX2a = fX2 + fX2b
        fYa = fY + fYb

        def chain_f(b, i, t):
            pc = t % 2
            pl = t % 4
            an, bn, yn = f"fA{pc}", f"fB{pc}", f"fY{pl}"
            x2n = f"fX2{pl}"
            s4 = tiny_pool(f"F_s4{pc}")
            mu4 = tiny_pool(f"F_mu4{pc}")
            op("vector", [f"F_s4{pc}"], [f"F_mu4{pc}"],
               lambda h: h.tensor_scalar(mu4, s4, 1.0 / D, None, ALU.mult))
            nmu4 = tiny_pool(f"F_nmu4{pc}")
            op("vector", [f"F_s4{pc}"], [f"F_nmu4{pc}"],
               lambda h: h.tensor_scalar(nmu4, s4, -1.0 / D, None, ALU.mult))
            v4 = tiny_pool(f"F_v4{pc}")
            op("scalar", [yn, f"F_nmu4{pc}"], [an, f"F{pc}_acc"],
               lambda h, pc=pc, pl=pl: h.activation(fA[pc], fYa[pl][:, :],
                                                    AF.Square, bias=nmu4,
                                                    accum_out=v4))
            r4 = rstd(v4, D, EPS_OUT, f"F{pc}")
            nm4 = tiny_pool(f"F_nm4{pc}")
            op("vector", [f"F_mu4{pc}", f"F{pc}_r"], [f"F_nm4{pc}"],
               lambda h: h.tensor_scalar(nm4, mu4, r4, -1.0, ALU.mult,
                                         ALU.mult))
            op("scalar", [yn, f"F{pc}_r", f"F_nm4{pc}"], [bn],
               lambda h, pc=pc, pl=pl: h.activation(fB[pc], fYa[pl][:, :],
                                                    AF.Identity, scale=r4,
                                                    bias=nm4))
            on = f"fOU{pc}"
            op("vector", [bn, x2n], [on],
               lambda h, pc=pc, pl=pl: h.tensor_tensor(
                   fOU[pc][:, :], fB[pc], fX2a[pl], ALU.add))
            dma("scalar", dou[pc], [on], [],
                out_t[b, i * 128:(i + 1) * 128, :], fOU[pc])

        pend = []

        def emit_load(t):
            b, i, _ = items_f[t]
            pl = t % 4
            dma("sync", dx[pl], [], [f"fX2{pl}"], fX2a[pl],
                spill[b, i * 128:(i + 1) * 128, :])

        def emit_clip(t):
            pc, pq = t % 2, t % 4
            v, nm = xnf(t)
            an = f"fA{pc}"
            op("vector", [nm, "s_g"], [an],
               lambda h, v=v, pc=pc: h.tensor_scalar(
                   fA[pc], v, s_g, CLIP_HI, ALU.mult, ALU.min))
            op("vector", [an], [f"fXQF{pq}"],
               lambda h, pc=pc, pq=pq: h.tensor_scalar(
                   fXQFa[pq], fA[pc], CLIP_LO, None, ALU.max))

        def cxb_load(t):
            b, i, _ = items_f[t]
            sl = t % 4
            dma("gpsimd", dxb[sl], [], [f"cXB{sl}"], cXB[sl],
                xnsp[b, i * 128:(i + 1) * 128, :])

        def cxb_tr(t):
            sl, pc = t % 4, t % 2
            vlt = vLT0 if pc == 0 else vLT1
            psn = f"psS{pc}"

            def tr(h, vlt=vlt, sl=sl):
                for c in range(NCH):
                    last = h.transpose(vlt[:, c * 128:(c + 1) * 128],
                                       cXB[sl][:, c * 128:(c + 1) * 128],
                                       idn_bf[:, :])
                return last

            op("tensor", [f"cXB{sl}", "idn_bf"], [psn], tr)
            v, nm = xnf(t)
            op("vector", [psn], [nm],
               lambda h, v=v, vlt=vlt: h.tensor_copy(v, vlt))

        for u in range(NPRE, min(NPRE + 4, NIT)):
            cxb_load(u)
        emit_load(0)
        emit_load(1)
        emit_clip(0)
        emit_clip(1)
        for b, i, t in items_f:
            pc, pq = t % 2, t % 4

            if len(pend) >= 2:
                chain_f(*pend.pop(0))

            psY, pyn = (psA, "psA") if pc == 0 else (psB, "psB")

            def ymm(h, psY=psY, pq=pq):
                for uh in range(2):
                    for dc in range(NCH):
                        last = h.matmul(
                            psY[:, uh * 512:(uh + 1) * 512],
                            fXQFa[pq][:, dc * 128:(dc + 1) * 128],
                            WQS[:, dc * D + uh * 512:dc * D + (uh + 1) * 512],
                            start=(dc == 0), stop=(dc == NCH - 1))
                return last

            op("tensor", [f"fXQF{pq}", "WQS"], [pyn], ymm)
            if t + 2 < NIT:
                emit_load(t + 2)
                emit_clip(t + 2)
            if NPRE <= t + 4 < NIT:
                cxb_tr(t + 4)
            if NPRE + 4 <= t + 8 < NIT:
                cxb_load(t + 8)
            yn = f"fY{pq}"
            s4 = tiny_pool(f"F_s4{pc}")
            op("scalar", [pyn, "gb_g"], [yn, f"F_s4{pc}"],
               lambda h, psY=psY, pq=pq, s4=s4: h.activation(
                   fYa[pq], psY[:, :], AF.Identity, scale=gb_g,
                   accum_out=s4))
            pend.append((b, i, t))

        for it in pend:
            chain_f(*it)

        def fin(h):
            h.wait_ge(dou[0].sem, dou[0].n * 16)
            h.wait_ge(dou[1].sem, dou[1].n * 16)

        sec("scalar", fin)

    return nc


def kernel(**inputs):
    x = np.ascontiguousarray(np.asarray(inputs["L0_attn_x"], dtype=np.float32))
    args = {
        "w1": np.ascontiguousarray(np.asarray(inputs["L0_attn_W1"], np.float32)),
        "w2": np.ascontiguousarray(np.asarray(inputs["L0_attn_W2"], np.float32)),
        "w3": np.ascontiguousarray(np.asarray(inputs["L0_attn_W3"], np.float32)),
        "w4": np.ascontiguousarray(np.asarray(inputs["L0_attn_W4"], np.float32)),
        "wf": np.ascontiguousarray(np.asarray(inputs["L0_ffn_weight"], np.float32)),
        "b1": np.ascontiguousarray(np.asarray(inputs["L0_attn_b1"], np.float32)),
        "b2": np.ascontiguousarray(np.asarray(inputs["L0_attn_b2"], np.float32)),
        "b3": np.ascontiguousarray(np.asarray(inputs["L0_attn_b3"], np.float32)),
        "b4": np.ascontiguousarray(np.asarray(inputs["L0_attn_b4"], np.float32)),
        "gam": np.ascontiguousarray(np.asarray(inputs["L0_ffn_ln_gamma"], np.float32)),
        "bet": np.ascontiguousarray(np.asarray(inputs["L0_ffn_ln_beta"], np.float32)),
    }
    n_cores = 8
    nb = x.shape[0] // n_cores
    nc = build_nc(B=nb, T=x.shape[1])
    in_maps = []
    for c in range(n_cores):
        m = {"x": np.ascontiguousarray(x[c * nb:(c + 1) * nb])}
        m.update(args)
        in_maps.append(m)
    res = run_bass_kernel_spmd(nc, in_maps, core_ids=list(range(n_cores)))
    return np.concatenate([r["out"] for r in res.results], axis=0)


# version 37
# speedup vs baseline: 1.0529x; 1.0144x over previous
import numpy as np
import concourse.bass as bass
import concourse.mybir as mybir
from concourse.bass_utils import run_bass_kernel_spmd

F32 = mybir.dt.float32
BF16 = mybir.dt.bfloat16
U8 = mybir.dt.uint8
AF = mybir.ActivationFunctionType
ALU = mybir.AluOpType
AX = mybir.AxisListType

D = 1024
NCH = 8
QBF = 128.0
CLIP_HI = float(np.float32(QBF - 1e-5))
CLIP_LO = float(np.float32(-QBF + 1e-5))
ATT_SCALE = 0.03125
EPS_OUT = 1e-5
EPS_IN = 1e-3
EPS_W = 1e-5


class E:

    def __init__(self, sem):
        self.sem = sem
        self.n = 0
        self.seen = {}
        self.h = None

    def w(self, tok):
        if tok is None:
            return
        sem, val = tok
        if self.seen.get(sem.name, 0) >= val:
            return
        self.h.wait_ge(sem, val)
        self.seen[sem.name] = val

    def wmany(self, toks):
        for t in toks:
            self.w(t)

    def d(self, inst):
        inst.then_inc(self.sem, 1)
        self.n += 1
        return (self.sem, self.n)

    def t(self):
        return (self.sem, self.n)


class Buf:

    def __init__(self, name):
        self.name = name
        self.lw = None
        self.lr = []

    def rd_waits(self):
        return [self.lw]

    def wr_waits(self):
        return ([self.lw] if self.lw else []) + list(self.lr)

    def note_rd(self, tok):
        self.lr.append(tok)

    def note_wr(self, tok):
        self.lw = tok
        self.lr = []


def build_nc(B=2, T=2048, debug=False):
    NT = T // 128
    NSL = T // 512
    nc = bass.Bass("TRN2", target_bir_lowering=False)

    x_in = nc.dram_tensor("x", [B, T, D], F32, kind="ExternalInput").ap()
    out_t = nc.dram_tensor("out", [B, T, D], F32, kind="ExternalOutput").ap()
    w_hbm = {}
    for nm in ("w1", "w2", "w3", "w4", "wf"):
        w_hbm[nm] = nc.dram_tensor(nm, [D, D], F32, kind="ExternalInput").ap()
    vecs = {}
    for nm in ("b1", "b2", "b3", "b4", "gam", "bet"):
        vecs[nm] = nc.dram_tensor(nm, [D], F32, kind="ExternalInput").ap()
    spill = nc.dram_tensor("spill", [B, T, D], F32).ap()
    xnsp = nc.dram_tensor("xnsp", [B, T, D], BF16).ap()
    sc128 = nc.dram_tensor("sc128", [128], F32).ap()
    sc128b = nc.dram_tensor("sc128b", [128], F32).ap()
    sc1a = nc.dram_tensor("sc1a", [1, 1], F32).ap()
    lrowb = nc.dram_tensor("lrowb_d", [1, 1], F32).ap()
    cc_in = nc.dram_tensor("cc_in", [1, 1], F32).ap()
    b34d = nc.dram_tensor("b34d", [D], F32).ap()
    cc_out = nc.dram_tensor("cc_out", [1, 1], F32).ap()

    def sb(name, shape, dt):
        return nc.alloc_sbuf_tensor(name, shape, dt).ap()

    QF = sb("QF", [128, NCH * T], BF16)
    if NT * D >= 8 * 1024:
        w3flat, w3nm = None, "VT"
    else:
        w3flat = sb("w3tS", [128, 8 * 1024], BF16)
        w3nm = "w3tS"
    KF = sb("KF", [128, NCH * T], BF16)
    VT = sb("VT", [128, NT * D], BF16)
    W4S = sb("W4S", [128, NCH * D], BF16)
    WQS = sb("WQS", [128, NCH * D], BF16)
    idn_bf = sb("idn_bf", [128, 128], BF16)
    b1c = sb("b1c", [128, NCH], F32)
    b2c = sb("b2c", [128, NCH], F32)
    b3c = sb("b3c", [128, NCH], BF16)
    lrow = sb("lrow", [1, 128], F32)
    l11 = sb("l11", [1, 1], F32)
    lrowB = sb("lrowB", [1, 128], F32)
    l11b = sb("l11b", [1, 1], F32)
    TINY = sb("TINY", [128, 64], F32)
    tiny_i = [0]

    def tiny():
        c = tiny_i[0]
        tiny_i[0] += 1
        assert c < 64
        return TINY[:, c:c + 1]

    rbeta_bc = tiny()
    betaw_bc = tiny()
    gam_g = tiny()
    s_g = tiny()
    gb_g = tiny()
    gmax = [tiny(), tiny()]

    ARENA_BYTES = 75776
    arena = sb("arena", [128, ARENA_BYTES], U8)

    def ar(off, nelem, dt, width):
        return arena[:, off:off + nelem * width].bitcast(dt)

    def arf(off, n):
        return ar(off, n, F32, 4)

    def arb(off, n):
        return ar(off, n, BF16, 2)

    pS0 = arf(0, 4096)
    pS1 = arf(16384, 4096)
    pTH = arf(32768, 4096)
    pRH = arf(49152, 4096)
    pIOTA = arf(73728, 128)
    idn_f = arf(74240 // 4 * 0 + 74240, 128)

    LN1F = arb(0, 16384)
    pXT = [arf(32768, 1024), arf(36864, 1024), arf(40960, 1024)]
    pSCR = [arf(45056, 1024)]
    pLNT = [arb(49152, 1024), arb(51200, 1024)]
    pB3 = arf(53248, 1024)
    WBUF = arb(57344, 8192)

    lP = arb(0, 2048)
    lPT = arb(4096, 2048)
    lOT = arb(8192, 1024)
    lOF = arb(10240, 1024)
    lZ = arf(12288, 1024)
    lXT = [arf(16384, 1024), arf(20480, 1024)]
    lX2 = [arf(24576, 1024), arf(28672, 1024)]
    lA = [arf(32768, 1024), arf(36864, 1024)]
    lBu = [arf(40960, 1024), arf(45056, 1024)]
    lB4 = arf(49152, 1024)
    lGAMb = arb(53248, 1024)
    lBETb = arb(55296, 1024)
    lW16 = [arb(40960, 1024), arb(45056, 1024)]
    lXN1 = [arb(32768, 1024), arb(36864, 1024)]
    lXN2 = [arb(40960 + 2048, 1024), arb(45056 + 2048, 1024)]

    fX2 = [arf(0, 1024), arf(4096, 1024)]
    cXB = [arb(8192, 1024), arb(10240, 1024),
           arb(61440, 1024), arb(63488, 1024)]
    fXQF = [arb(16384, 1024), arb(18432, 1024)]
    fY = [arf(20480, 1024), arf(24576, 1024)]
    fOU = [arf(28672, 1024), arf(32768, 1024)]
    fA = [arf(36864, 1024), arf(40960, 1024)]
    fB = [arf(45056, 1024), arf(49152, 1024)]
    fX2b = [arf(53248, 1024), arf(57344, 1024)]
    fYb = [arf(65536, 1024), arf(69632, 1024)]
    cXN = [arb(65536, 1024), arb(67584, 1024),
           arb(69632, 1024), arb(71680, 1024)]
    fXQFa = [arb(16384, 1024), arb(18432, 1024),
             arb(12288, 1024), arb(14336, 1024)]

    psS = nc.psum_tensor("psS", [128, 2048], F32).__enter__()
    psA = nc.psum_tensor("psA", [128, 1024], F32).__enter__()
    psB = nc.psum_tensor("psB", [128, 1024], F32).__enter__()
    psS0 = psS[:, 0:1024]
    psS1 = psS[:, 1024:2048]
    vLT0 = psS[:, 0:512].bitcast(BF16)
    vLT1 = psS[:, 1024:1536].bitcast(BF16)
    vPT = psA[:, :].bitcast(BF16)
    vPTB = psB[:, :].bitcast(BF16)
    vOF = psB[:, 0:512].bitcast(BF16)

    def sem(n):
        return nc.semaphore(n).__enter__()

    PE = E(sem("s_pe"))
    VE = E(sem("s_ve"))
    SE = E(sem("s_se"))
    GE = E(sem("s_ge"))
    SY = E(sem("s_sy"))
    ENGS = {"tensor": PE, "vector": VE, "scalar": SE, "gpsimd": GE, "sync": SY}

    class DS:
        def __init__(self, name):
            self.sem = sem(name)
            self.n = 0

        def go(self, q, out, in_, **kw):
            q.h.dma_start(out=out, in_=in_, **kw).then_inc(self.sem, 16)
            self.n += 1
            return (self.sem, self.n * 16)

    dx = [DS("d_x0"), DS("d_x1"), DS("d_x2"), DS("d_x3")]
    dxn = [DS("d_xn0"), DS("d_xn1"), DS("d_xn2"), DS("d_xn3")]
    dw = {nm: DS("d_" + nm) for nm in ("w1", "w2", "w3")}
    dwc = [DS(f"d_wc{c}") for c in range(NCH)]
    dwf = [DS(f"d_wf{c}") for c in range(NCH)]
    dxb = [DS(f"d_xb{c}") for c in range(4)]
    dgm = DS("d_gm")
    dpm = DS("d_pm")
    dbc = DS("d_bc")
    dsp = [DS("d_sp0"), DS("d_sp1")]
    dxs = [DS("d_xs0"), DS("d_xs1")]
    dou = [DS("d_ou0"), DS("d_ou1")]
    dcc = DS("d_cc")
    dcc2 = DS("d_cc2")
    ccs = sem("s_cc")

    bufs = {}

    def bget(name):
        if name not in bufs:
            bufs[name] = Buf(name)
        return bufs[name]

    cur_blk = [None]

    def sec(ename, body):
        eng = ENGS[ename]
        getattr(cur_blk[0], ename)(lambda h: (setattr(eng, "h", h), body(h))[0])

    def op(ename, reads, writes, emit):
        eng = ENGS[ename]
        tok_out = [None]

        def body(h):
            for r in reads:
                eng.wmany(bget(r).rd_waits())
            for wn in writes:
                eng.wmany(bget(wn).wr_waits())
            tok = eng.d(emit(h))
            for r in reads:
                bget(r).note_rd(tok)
            for wn in writes:
                bget(wn).note_wr(tok)
            tok_out[0] = tok

        sec(ename, body)
        return tok_out[0]

    def dma(qname, stream, reads, writes, out, in_, **kw):
        eng = ENGS[qname]
        tok_out = [None]

        def body(h):
            for r in reads:
                eng.wmany(bget(r).rd_waits())
            for wn in writes:
                eng.wmany(bget(wn).wr_waits())
            tok = stream.go(eng, out, in_, **kw)
            for r in reads:
                bget(r).note_rd(tok)
            for wn in writes:
                bget(wn).note_wr(tok)
            tok_out[0] = tok

        sec(qname, body)
        return tok_out[0]

    def dma_batch(qname, stream, entries):
        eng = ENGS[qname]

        def body(h):
            for reads, writes, out, in_, kw in entries:
                for r in reads:
                    eng.wmany(bget(r).rd_waits())
                for wn in writes:
                    eng.wmany(bget(wn).wr_waits())
                stream.go(eng, out, in_, **kw)
            tok = (stream.sem, stream.n * 16)
            for reads, writes, out, in_, kw in entries:
                for r in reads:
                    bget(r).note_rd(tok)
                for wn in writes:
                    bget(wn).note_wr(tok)

        sec(qname, body)

    eps_ap = {}

    def rstd(vsum, n, eps, tag):
        lt = tiny_pool(tag + "_lt")
        r = tiny_pool(tag + "_r")
        op("scalar", [tag + "_acc", "eps_c"], [tag + "_lt"],
           lambda h: h.activation(lt, vsum, AF.Ln, scale=1.0 / n,
                                  bias=eps_ap[eps]))
        op("scalar", [tag + "_lt"], [tag + "_r"],
           lambda h: h.activation(r, lt, AF.Exp, scale=-0.5))
        return r

    from concourse.hw_specs import get_activation_tables
    _act_set = list(get_activation_tables(nc.m.arch)).index(
        "natural_log_exp_and_others")

    def preload_tables():
        def body(h):
            h.add_instruction(mybir.InstLoadActFuncSet(
                name=nc.get_next_instruction_name(),
                act_func_set_id=_act_set, ins=[], outs=[]))

        sec("scalar", body)

    tiny_slots = {}

    def tiny_pool(tag):
        if tag not in tiny_slots:
            tiny_slots[tag] = tiny()
        return tiny_slots[tag]

    ones_like = None

    def wchunk_load(wname):
        for c in range(NCH):
            dma("gpsimd", dwc[c], [f"WB{c}"], [f"WB{c}"],
                WBUF[:, c * 1024:(c + 1) * 1024],
                w_hbm[wname][c * 128:(c + 1) * 128, :])

    def emit_prologue():

            entries = [
                ([], ["W4S"], W4S.rearrange("p (c u) -> p c u", c=NCH),
                 w_hbm["w4"].rearrange("(c p) u -> p c u", p=128), {}),
            ]
            for c in range(NCH):
                entries.append(([], ["b1c"], b1c[:, c:c + 1],
                                vecs["b1"][c * 128:(c + 1) * 128].rearrange(
                                    "(p o) -> p o", o=1), {}))
                entries.append(([], ["b2c"], b2c[:, c:c + 1],
                                vecs["b2"][c * 128:(c + 1) * 128].rearrange(
                                    "(p o) -> p o", o=1), {}))
                entries.append(([], ["b3c"], b3c[:, c:c + 1],
                                vecs["b3"][c * 128:(c + 1) * 128].rearrange(
                                    "(p o) -> p o", o=1), {}))
            dma_batch("gpsimd", dpm, entries)
            for v in (EPS_OUT, EPS_IN):
                eps_ap[v] = tiny()
                op("vector", [], ["eps_c"],
                   lambda h, v=v: h.memset(eps_ap[v], v))
            op("gpsimd", [], ["iota"],
               lambda h: h.iota(pIOTA, [[1, 128]], base=0, channel_multiplier=-1,
                                allow_small_or_imprecise_dtypes=True))
            op("vector", ["iota"], ["idn_bf"],
               lambda h: h.tensor_scalar(idn_bf[:, :], pIOTA, 0.0, None, ALU.is_equal))
            op("vector", ["iota"], ["idn_f"],
               lambda h: h.tensor_scalar(idn_f[:, :], pIOTA, 0.0, None, ALU.is_equal))



    def phase_P(b, w1_prefetched=False):
        with nc.Block() as blk:
            cur_blk[0] = blk
            preload_tables()
            if b == 0:
                emit_prologue()
            if not w1_prefetched:
                wchunk_load("w1")

            slots = [(psA, 0, "psA0"), (psA, 512, "psA1"),
                     (psB, 0, "psB0"), (psB, 512, "psB1")]

            def emit_tile(i):
                p = i % 2
                px = i % 3
                xb, sc = f"pXT{px}", "pSCR0"
                dma("sync", dx[px], [], [xb], pXT[px],
                    x_in[b, i * 128:(i + 1) * 128, :])
                s1 = tiny_pool("P_s1")
                op("scalar", [xb], [sc, "P_s1"],
                   lambda h: h.activation(pSCR[0], pXT[px], AF.Identity,
                                          accum_out=s1))
                mu1 = tiny_pool("P_mu1")
                op("vector", ["P_s1"], ["P_mu1"],
                   lambda h: h.tensor_scalar(mu1, s1, 1.0 / D, None, ALU.mult))
                nm1 = tiny_pool("P_nm1")
                op("vector", ["P_s1"], ["P_nm1"],
                   lambda h: h.tensor_scalar(nm1, s1, -1.0 / D, None, ALU.mult))
                v1 = tiny_pool("P_v1")
                op("scalar", [xb, "P_nm1"], [sc, "P_acc"],
                   lambda h: h.activation(pSCR[0], pXT[px], AF.Square, bias=nm1,
                                          accum_out=v1))
                r1 = rstd(v1, D, EPS_OUT, "P")
                lt = f"pLNT{p}"
                op("vector", [xb, "P_mu1", "P_r"], [lt],
                   lambda h: h.tensor_scalar(pLNT[p], pXT[px], mu1, r1,
                                             ALU.subtract, ALU.mult))
                vlt = vLT0 if p == 0 else vLT1
                psn = f"psS{p}"

                def tr(h, vlt=vlt, p=p):
                    for c in range(NCH):
                        last = h.transpose(vlt[:, c * 128:(c + 1) * 128],
                                           pLNT[p][:, c * 128:(c + 1) * 128],
                                           idn_bf[:, :])
                    return last

                op("tensor", [lt, "idn_bf"], [psn], tr)
                op("vector", [psn], ["LN1F"],
                   lambda h, vlt=vlt, i=i: h.tensor_copy(
                       LN1F[:, 0:NCH * T].rearrange(
                           "p (c t) -> p c t", c=NCH)[
                           :, :, i * 128:(i + 1) * 128],
                       vlt.rearrange("p (c t) -> p c t", c=NCH)))

            def proj_pass(groups, kind, dst_ap=None, dst_name=None,
                          bias_c=None, bias_name=None, split=False):
                gs = [(g, slots[j]) for j, g in enumerate(groups)]

                wsrc = WBUF if (kind == "fm" or b == 0) else W4S
                wnm = (lambda dc: f"WB{dc}") if (kind == "fm" or b == 0) \
                    else (lambda dc: "W4S")

                def one(h, g, ps, of, dc):
                    if kind == "fm":
                        k, ec = g
                        return h.matmul(
                            ps[:, of:of + 512],
                            WBUF[:, dc * 1024 + ec * 128:dc * 1024 + (ec + 1) * 128],
                            LN1F[:, dc * T + k * 512:dc * T + (k + 1) * 512],
                            start=(dc == 0), stop=(dc == NCH - 1))
                    tc, uh = g
                    return h.matmul(
                        ps[:, of:of + 512],
                        LN1F[:, dc * T + tc * 128:dc * T + (tc + 1) * 128],
                        wsrc[:, dc * 1024 + uh * 512:dc * 1024 + (uh + 1) * 512],
                        start=(dc == 0), stop=(dc == NCH - 1))

                if split:
                    for dc in range(NCH):
                        def mmc(h, dc=dc):
                            for g, (ps, of, pn) in gs:
                                last = one(h, g, ps, of, dc)
                            return last

                        op("tensor", [wnm(dc), "LN1F"],
                           [pn for _, (_, _, pn) in gs], mmc)
                else:
                    for g, (ps, of, pn) in gs:
                        def mm(h, g=g, ps=ps, of=of):
                            for dc in range(NCH):
                                last = one(h, g, ps, of, dc)
                            return last

                        op("tensor", sorted({wnm(c) for c in range(NCH)})
                           + ["LN1F"], [pn], mm)
                for g, (ps, of, pn) in gs:
                    if kind == "fm":
                        k, ec = g
                        op("scalar", [pn, bias_name], [dst_name],
                           lambda h, ps=ps, of=of, ec=ec, k=k:
                           h.activation(
                               dst_ap[:, ec * T + k * 512:ec * T + (k + 1) * 512],
                               ps[:, of:of + 512], AF.Identity,
                               bias=bias_c[:, ec:ec + 1]))
                    else:
                        tc, uh = g
                        op("vector", [pn], ["VT"],
                           lambda h, ps=ps, of=of, tc=tc, uh=uh:
                           h.tensor_copy(
                               VT[:, tc * D + uh * 512:tc * D + (uh + 1) * 512],
                               ps[:, of:of + 512]))

            qgroups = [(k, ec) for k in range(NSL) for ec in range(NCH)]
            nq = len(qgroups)
            ntile_done = 0
            for j in range(0, nq, 4):
                need_k = max(k for k, _ in qgroups[j:j + 4])
                while ntile_done < min((need_k + 1) * 4, NT):
                    emit_tile(ntile_done)
                    ntile_done += 1
                proj_pass(qgroups[j:j + 4], "fm", QF, "QF", b1c, "b1c",
                          split=(j + 4 >= nq))
            while ntile_done < NT:
                emit_tile(ntile_done)
                ntile_done += 1
            wchunk_load("w2")
            for j in range(0, nq, 4):
                proj_pass(qgroups[j:j + 4], "fm", KF, "KF", b2c, "b2c",
                          split=(j == 0 or j + 4 >= nq))
            if b == 0:
                wchunk_load("w3")
                w3f = VT if w3flat is None else w3flat
                w3t = w3f[:, 0:8 * 1024].rearrange("p (es d) -> p es d", es=8)
                for dc in range(NCH):
                    pcw = dc % 2
                    vlt = vLT0 if pcw == 0 else vLT1
                    psn = f"psS{pcw}"

                    def trw(h, vlt=vlt, dc=dc):
                        for es in range(NCH):
                            last = h.transpose(
                                vlt[:, es * 128:(es + 1) * 128],
                                WBUF[:, dc * 1024 + es * 128:dc * 1024 + (es + 1) * 128],
                                idn_bf[:, :])
                        return last

                    op("tensor", [f"WB{dc}", "idn_bf"], [psn], trw)
                    if pcw == 0:
                        op("scalar", [psn], [w3nm],
                           lambda h, vlt=vlt, dc=dc: h.activation(
                               w3t[:, :, dc * 128:(dc + 1) * 128],
                               vlt.rearrange("p (es d) -> p es d", es=8),
                               AF.Identity))
                    else:
                        op("vector", [psn], [w3nm],
                           lambda h, vlt=vlt, dc=dc: h.tensor_copy(
                               w3t[:, :, dc * 128:(dc + 1) * 128],
                               vlt.rearrange("p (es d) -> p es d", es=8)))
                def b34mm(h):
                    for uh in range(2):
                        for es in range(NCH):
                            last = h.matmul(
                                psS[0:1, uh * 512:(uh + 1) * 512],
                                b3c[:, es:es + 1],
                                W4S[:, es * D + uh * 512:es * D + (uh + 1) * 512],
                                start=(es == 0), stop=(es == NCH - 1))
                    return last

                op("tensor", ["b3c", "W4S"], ["psS0"], b34mm)
                dma("gpsimd", dbc, [], ["pB3"], pB3[0:1, :],
                    vecs["b4"].unsqueeze(0))
                op("vector", ["psS0", "pB3"], ["pB3"],
                   lambda h: h.tensor_tensor(pB3[0:1, :], psS[0:1, 0:1024],
                                             pB3[0:1, :], ALU.add))
                dma("gpsimd", dbc, ["pB3"], ["b34d"],
                    b34d.rearrange("(a u) -> a u", a=1), pB3[0:1, :])
                for dc in range(NCH):
                    psY, pyn = (psA, "psA0") if dc % 2 == 0 else (psB, "psB0")

                    def w34mm(h, psY=psY, dc=dc):
                        for uh in range(2):
                            for es in range(NCH):
                                last = h.matmul(
                                    psY[:, uh * 512:(uh + 1) * 512],
                                    w3f[:, es * 1024 + dc * 128:es * 1024 + (dc + 1) * 128],
                                    W4S[:, es * D + uh * 512:es * D + (uh + 1) * 512],
                                    start=(es == 0), stop=(es == NCH - 1))
                        return last

                    op("tensor", [w3nm, "W4S"], [pyn, "psA1" if dc % 2 == 0
                                                 else "psB1"], w34mm)
                    if dc % 2 == 0:
                        op("scalar", [pyn, "psA1"], [f"WB{dc}"],
                           lambda h, psY=psY, dc=dc: h.activation(
                               WBUF[:, dc * 1024:(dc + 1) * 1024], psY[:, :],
                               AF.Identity))
                    else:
                        op("vector", [pyn, "psB1"], [f"WB{dc}"],
                           lambda h, psY=psY, dc=dc: h.tensor_copy(
                               WBUF[:, dc * 1024:(dc + 1) * 1024], psY[:, :]))
            vgroups = [(tc, uh) for tc in range(NT) for uh in range(2)]
            for j in range(0, len(vgroups), 4):
                proj_pass(vgroups[j:j + 4], "v", split=(b == 0 and j == 0))
            if b == 0:
                for dc in range(NCH):
                    if dc % 2 == 0:
                        op("scalar", [f"WB{dc}"], ["W4S"],
                           lambda h, dc=dc: h.activation(
                               W4S[:, dc * 1024:(dc + 1) * 1024],
                               WBUF[:, dc * 1024:(dc + 1) * 1024],
                               AF.Identity))
                    else:
                        op("vector", [f"WB{dc}"], ["W4S"],
                           lambda h, dc=dc: h.tensor_copy(
                               W4S[:, dc * 1024:(dc + 1) * 1024],
                               WBUF[:, dc * 1024:(dc + 1) * 1024]))

    gm_tick = [0]

    def phase_L(b):
        with nc.Block() as blk:
            cur_blk[0] = blk
            preload_tables()
            dma_batch("gpsimd", dbc, [
                (["b34d"], ["lB4"], lB4,
                 b34d.unsqueeze(0).partition_broadcast(128), {}),
                ([], ["lGAM"], lGAMb,
                 vecs["gam"].unsqueeze(0).partition_broadcast(128), {}),
                ([], ["lBET"], lBETb,
                 vecs["bet"].unsqueeze(0).partition_broadcast(128), {}),
            ])

            NKH = max(1, T // 1024)
            KW = T // NKH

            def chain_rest(q):
                p = q % 2
                A, Bu = lA[p], lBu[p]
                an, bn = f"lA{p}", f"lBu{p}"
                xn_, x2n = f"lXT{p}", f"lX2{p}"
                sz = tiny_pool("L_sz")
                muz = tiny_pool("L_muz")
                op("vector", ["L_sz"], ["L_muz"],
                   lambda h: h.tensor_scalar(muz, sz, 1.0 / D, None, ALU.mult))
                nmz = tiny_pool("L_nmz")
                op("vector", ["L_sz"], ["L_nmz"],
                   lambda h: h.tensor_scalar(nmz, sz, -1.0 / D, None, ALU.mult))
                vz = tiny_pool("L_vz")
                op("scalar", ["lZ", "L_nmz"], [an, "L_acc"],
                   lambda h: h.activation(A, lZ[:, :], AF.Square, bias=nmz,
                                          accum_out=vz))
                rz = rstd(vz, D, EPS_OUT, "L")
                op("vector", ["lZ", "L_muz", "L_r"], [an],
                   lambda h: h.tensor_scalar(A, lZ[:, :], muz, rz,
                                             ALU.subtract, ALU.mult))
                s2 = tiny_pool("L_s2")
                op("vector", [an, xn_], [x2n, "L_s2"],
                   lambda h: h.scalar_tensor_tensor(
                       lX2[p][:, :], A, 1.0, lXT[p], ALU.bypass, ALU.add,
                       accum_out=s2))
                dma("scalar", dsp[p], [x2n], [],
                    spill[b, q * 128:(q + 1) * 128, :], lX2[p])
                mu2 = tiny_pool("L_mu2")
                op("vector", ["L_s2"], ["L_mu2"],
                   lambda h: h.tensor_scalar(mu2, s2, 1.0 / D, None, ALU.mult))
                nm2 = tiny_pool("L_nm2")
                op("vector", ["L_s2"], ["L_nm2"],
                   lambda h: h.tensor_scalar(nm2, s2, -1.0 / D, None, ALU.mult))
                v2 = tiny_pool("L_v2")
                op("scalar", [x2n, "L_nm2"], [an, "L2_acc"],
                   lambda h: h.activation(A, lX2[p][:, :], AF.Square, bias=nm2,
                                          accum_out=v2))
                r2 = rstd2(v2, D, EPS_OUT, "L2")
                op("vector", [x2n, "L_mu2", "L2_r"], [an],
                   lambda h: h.tensor_scalar(A, lX2[p][:, :], mu2, r2,
                                             ALU.subtract, ALU.mult))
                d3 = tiny_pool("L_d3")
                op("vector", ["L_s2", "L_mu2"], ["L_d3"],
                   lambda h: h.tensor_scalar(d3, s2, 1.0 / D, mu2,
                                             ALU.mult, ALU.subtract))
                mu3 = tiny_pool("L_mu3")
                op("vector", ["L_d3", "L2_r"], ["L_mu3"],
                   lambda h: h.tensor_scalar(mu3, d3, r2, None, ALU.mult))
                nm3 = tiny_pool("L_nm3")
                op("vector", ["L_mu3"], ["L_nm3"],
                   lambda h: h.tensor_scalar(nm3, mu3, -1.0, None, ALU.mult))
                v3 = tiny_pool("L_v3")
                op("scalar", [an, "L_nm3"], [bn, "L3_acc"],
                   lambda h: h.activation(Bu, A, AF.Square, bias=nm3,
                                          accum_out=v3))
                r3 = rstd2(v3, D, EPS_IN, "L3")
                op("vector", [an, "L_mu3", "L3_r"], [bn],
                   lambda h: h.tensor_scalar(lW16[p], A, mu3, r3,
                                             ALU.subtract, ALU.mult))
                op("vector", [bn, "lGAM"], [an],
                   lambda h: h.tensor_tensor(lXN1[p], lW16[p], lGAMb, ALU.mult))
                op("vector", [an, "lBET"], [bn],
                   lambda h: h.tensor_tensor(lXN2[p], lXN1[p], lBETb, ALU.add))
                mxq = tiny_pool("L_mxq")
                op("vector", [bn], ["L_mxq"],
                   lambda h: h.tensor_reduce(mxq, lXN2[p], AX.X, ALU.max,
                                             apply_absolute_value=True))
                gi = gm_tick[0]
                gm_tick[0] += 1
                if gi == 0:
                    op("vector", ["L_mxq"], ["gmax"],
                       lambda h: h.tensor_copy(gmax[0], mxq))
                else:
                    op("vector", ["L_mxq", "gmax"], ["gmax"],
                       lambda h, gi=gi: h.tensor_tensor(
                           gmax[gi % 2], gmax[(gi - 1) % 2], mxq, ALU.max))
                dma("gpsimd", dxs[p], [bn], [],
                    xnsp[b, q * 128:(q + 1) * 128, :], lXN2[p])

            def emit_ofz(q):
                psY, pyn = (psA, "psA") if q % 2 == 0 else (psB, "psB")
                rs = tiny_pool("L_rs")
                sz = tiny_pool("L_sz")
                op("vector", [pyn, "lB4", "L_rs"], ["lZ", "L_sz"],
                   lambda h, psY=psY: h.scalar_tensor_tensor(
                       lZ[:, :], psY[:, :], rs, lB4, ALU.mult, ALU.add,
                       accum_out=sz))

            def emit_scores(q):
                for half in range(NKH):
                    psn = f"psS{half}"

                    def mm(h, half=half, q=q):
                        for t2 in range(KW // 512):
                            tq = half * (KW // 512) + t2
                            for ec in range(NCH):
                                last = h.matmul(
                                    psS[:, tq * 512:(tq + 1) * 512],
                                    QF[:, ec * T + q * 128:ec * T + (q + 1) * 128],
                                    KF[:, ec * T + tq * 512:ec * T + (tq + 1) * 512],
                                    start=(ec == 0), stop=(ec == NCH - 1))
                        return last

                    op("tensor", ["QF", "KF"], [psn], mm)

            emit_scores(0)
            for q in range(NT):
                p = q % 2
                dma("sync", dx[p], [], [f"lXT{p}"], lXT[p],
                    x_in[b, q * 128:(q + 1) * 128, :])
                ng0 = tiny_pool("L_ng0")
                op("vector", ["psS0", "psS1"], ["L_ng0"],
                   lambda h: h.reduce_max(ng0, psS[:, 0:T], axis=AX.X,
                                          negate=True))
                ngs = tiny_pool("L_ngs")
                op("vector", ["L_ng0"], ["L_ngs"],
                   lambda h: h.tensor_scalar(ngs, ng0, ATT_SCALE, None,
                                             ALU.mult))

                if q > 0:
                    emit_ofz(q - 1)

                se0 = tiny_pool("L_se0")
                op("scalar", ["psS0", "psS1", "L_ngs"], ["lP", "L_se0"],
                   lambda h: h.activation(lP[:, 0:T], psS[:, 0:T], AF.Exp,
                                          scale=ATT_SCALE, bias=ngs,
                                          accum_out=se0))
                rs = tiny_pool("L_rs")
                op("vector", ["L_se0"], ["L_rs"],
                   lambda h: h.reciprocal(rs, se0))

                vp = vPT if p == 0 else vPTB
                psY, pyn = (psA, "psA") if p == 0 else (psB, "psB")

                def trp(h, vp=vp):
                    for j in range(NT):
                        last = h.transpose(vp[:, j * 128:(j + 1) * 128],
                                           lP[:, j * 128:(j + 1) * 128],
                                           idn_bf[:, :])
                    return last

                op("tensor", ["lP", "idn_bf"], [pyn], trp)
                op("scalar", [pyn], ["lPT"],
                   lambda h, vp=vp: h.activation(lPT[:, 0:NT * 128],
                                                 vp[:, 0:NT * 128],
                                                 AF.Identity))

                if q + 1 < NT:
                    emit_scores(q + 1)

                if q > 0:
                    chain_rest(q - 1)


                def pv(h, psY=psY):
                    for uh in range(2):
                        for tc in range(NT):
                            last = h.matmul(
                                psY[:, uh * 512:(uh + 1) * 512],
                                lPT[:, tc * 128:(tc + 1) * 128],
                                VT[:, tc * D + uh * 512:tc * D + (uh + 1) * 512],
                                start=(tc == 0), stop=(tc == NT - 1))
                    return last

                op("tensor", ["lPT", "VT"], [pyn], pv)

            emit_ofz(NT - 1)
            chain_rest(NT - 1)

            if b + 1 < B:
                wchunk_load("w1")
            else:
                gfin = gmax[(gm_tick[0] - 1) % 2]
                op("tensor", ["gmax", "idn_f"], ["psS0"],
                   lambda h: h.transpose(psS[0:1, 512:640], gfin,
                                         idn_f[:, :]))
                op("vector", ["psS0"], ["l11"],
                   lambda h: h.tensor_reduce(l11, psS[0:1, 512:640], AX.X,
                                             ALU.max))
                dma("sync", dgm, ["l11"], ["cc_in"], cc_in, l11)

                def coll_issue(h):
                    GE.wmany(bget("cc_in").rd_waits())
                    h.collective_compute(
                        "AllReduce", ALU.max, replica_groups=[list(range(8))],
                        ins=[cc_in.opt()], outs=[cc_out.opt()]).then_inc(ccs, 1)

                sec("gpsimd", coll_issue)

            def drain_sp(h):
                h.wait_ge(dsp[0].sem, dsp[0].n * 16)
                h.wait_ge(dsp[1].sem, dsp[1].n * 16)

            sec("scalar", drain_sp)

            def drain_xs(h):
                h.wait_ge(dxs[0].sem, dxs[0].n * 16)
                h.wait_ge(dxs[1].sem, dxs[1].n * 16)

            sec("gpsimd", drain_xs)

    def rstd2(vsum, n, eps, tag):
        return rstd(vsum, n, eps, tag)

    if debug:
        d_ln = nc.dram_tensor("d_ln", [128, NCH * T], BF16,
                              kind="ExternalOutput").ap()
        d_qf = nc.dram_tensor("d_qf", [128, NCH * T], BF16,
                              kind="ExternalOutput").ap()
        d_vt = nc.dram_tensor("d_vt", [128, NT * D], BF16,
                              kind="ExternalOutput").ap()
    ddma = DS("d_dbg")

    def dump_dbg():
        with nc.Block() as blk:
            cur_blk[0] = blk

            def body(h):
                h.dma_start(out=d_ln, in_=LN1F[:, 0:NCH * T]).then_inc(
                    ddma.sem, 16)
                h.dma_start(out=d_qf, in_=QF).then_inc(ddma.sem, 16)
                h.dma_start(out=d_vt, in_=VT).then_inc(ddma.sem, 16)
                h.wait_ge(ddma.sem, 48)

            sec("sync", body)

    for b in range(B):
        phase_P(b, w1_prefetched=(b > 0))
        if debug and b == 0:
            dump_dbg()
        phase_L(b)

    items_f = [(b, i, b * NT + i) for b in range(B) for i in range(NT)]
    NIT = len(items_f)
    NPRE = 8 if NIT >= 16 else max(1, NIT // 2)

    def xnf(t):
        if t < NT:
            return QF[:, t * 1024:(t + 1) * 1024], "QF"
        return KF[:, (t - NT) * 1024:(t - NT + 1) * 1024], "KF"

    with nc.Block() as blk:
        cur_blk[0] = blk
        preload_tables()

        def cxn_load(t):
            b, i, _ = items_f[t]
            sl = t % 4
            dma("sync" if sl % 2 == 0 else "scalar", dxn[sl], [],
                [f"cXN{sl}"], cXN[sl], xnsp[b, i * 128:(i + 1) * 128, :])

        def cxn_tr(t):
            sl, pc = t % 4, t % 2
            vlt = vLT0 if pc == 0 else vLT1
            psn = f"psS{pc}"

            def tr(h, vlt=vlt, sl=sl):
                for c in range(NCH):
                    last = h.transpose(vlt[:, c * 128:(c + 1) * 128],
                                       cXN[sl][:, c * 128:(c + 1) * 128],
                                       idn_bf[:, :])
                return last

            op("tensor", [f"cXN{sl}", "idn_bf"], [psn], tr)
            v, nm = xnf(t)
            if t % 2 == 0:
                op("scalar", [psn], [nm],
                   lambda h, v=v, vlt=vlt: h.activation(v, vlt, AF.Identity))
            else:
                op("vector", [psn], [nm],
                   lambda h, v=v, vlt=vlt: h.tensor_copy(v, vlt))

        for u in range(min(4, NPRE)):
            cxn_load(u)
        pS0c = pS0.rearrange("p (c u) -> p c u", c=4)
        pS1c = pS1.rearrange("p (c u) -> p c u", c=4)
        for c in range(4):
            dma("sync", dwf[c], [], [f"pS0c{c}"], pS0c[:, c, :],
                w_hbm["wf"][c * 128:(c + 1) * 128, :])
            dma("scalar", dwf[4 + c], [], [f"pS1c{c}"], pS1c[:, c, :],
                w_hbm["wf"][512 + c * 128:512 + (c + 1) * 128, :])

        for t in range(NPRE):
            cxn_tr(t)
            if t + 4 < NPRE:
                cxn_load(t + 4)
        asum = tiny_pool("asum")
        op("vector", [f"pS0c{c}" for c in range(4)], ["asum0"],
           lambda h: h.tensor_reduce(asum, pS0, AX.X, ALU.add,
                                     apply_absolute_value=True))
        a2t = tiny_pool("asum2")
        op("vector", [f"pS1c{c}" for c in range(4)], ["asum1"],
           lambda h: h.tensor_reduce(a2t, pS1, AX.X, ALU.add,
                                     apply_absolute_value=True))
        op("vector", ["asum0", "asum1"], ["asum"],
           lambda h: h.tensor_tensor(asum, asum, a2t, ALU.add))
        dma("gpsimd", dcc2, ["asum"], ["sc128b"],
            sc128b.rearrange("(p o) -> p o", o=1), asum)
        dma("gpsimd", dcc2, ["sc128b"], ["lrowB"], lrowB,
            sc128b.rearrange("(a b) -> a b", a=1))
        op("vector", ["lrowB"], ["l11b"],
           lambda h: h.tensor_reduce(l11b, lrowB, AX.X, ALU.add))
        dma("gpsimd", dcc2, ["l11b"], ["sc1a"], sc1a, l11b)
        dma("gpsimd", dcc2, ["sc1a"], ["betaw"], betaw_bc,
            sc1a.partition_broadcast(128))
        op("vector", ["betaw"], ["betaw"],
           lambda h: h.tensor_scalar(betaw_bc, betaw_bc, 1.0 / (D * D), None,
                                     ALU.mult))
        rb_t = tiny_pool("rb_t")
        op("vector", ["betaw"], ["rb_t"],
           lambda h: h.tensor_scalar(rb_t, betaw_bc, EPS_W, None, ALU.add))
        op("vector", ["rb_t"], ["rbeta"],
           lambda h: h.reciprocal(rbeta_bc, rb_t))

        for hn, S_h in (("h0", pS0), ("h1", pS1)):
            chn = [f"pS0c{c}" if hn == "h0" else f"pS1c{c}" for c in range(4)]
            op("vector", chn + ["rbeta"], ["pTH"],
               lambda h, S_h=S_h: h.tensor_scalar(pTH, S_h, rbeta_bc, None,
                                                  ALU.mult))
            op("vector", ["pTH"], ["pS0" if hn == "h0" else "pS1"],
               lambda h, S_h=S_h: h.tensor_scalar(S_h, pTH, 0.5, None,
                                                  ALU.is_gt))
            op("vector", ["pTH"], ["pRH"],
               lambda h: h.tensor_scalar(pRH, pTH, -0.5, None, ALU.is_lt))
            off = 0 if hn == "h0" else 4096
            op("vector", ["pS0" if hn == "h0" else "pS1", "pRH"], ["WQS"],
               lambda h, S_h=S_h, off=off: h.tensor_tensor(
                   WQS[:, off:off + 4096], S_h, pRH, ALU.subtract))

        def coll_wait(h):
            h.wait_ge(ccs, 1)

        sec("sync", coll_wait)
        dma("sync", dgm, [], ["gam_g"], gam_g,
            cc_out.partition_broadcast(128))
        tg = tiny_pool("C_tg")
        op("vector", ["gam_g"], ["C_tg"],
           lambda h: h.reciprocal(tg, gam_g))
        op("vector", ["C_tg"], ["s_g"],
           lambda h: h.tensor_scalar(s_g, tg, QBF, None, ALU.mult))
        op("vector", ["gam_g", "betaw"], ["gb_g"],
           lambda h: h.tensor_scalar(gb_g, gam_g, betaw_bc, 1.0 / QBF,
                                     ALU.mult, ALU.mult))

    with nc.Block() as blk:
        cur_blk[0] = blk
        preload_tables()

        def wsp(h):
            h.wait_ge(dsp[0].sem, dsp[0].n * 16)
            h.wait_ge(dsp[1].sem, dsp[1].n * 16)
            h.wait_ge(dxs[0].sem, dxs[0].n * 16)
            h.wait_ge(dxs[1].sem, dxs[1].n * 16)

        sec("sync", wsp)

        tick = [0]
        fX2a = fX2 + fX2b
        fYa = fY + fYb

        def chain_f(b, i, t):
            pc = t % 2
            pl = t % 4
            an, bn, yn = f"fA{pc}", f"fB{pc}", f"fY{pl}"
            x2n = f"fX2{pl}"
            s4 = tiny_pool(f"F_s4{pc}")
            mu4 = tiny_pool(f"F_mu4{pc}")
            op("vector", [f"F_s4{pc}"], [f"F_mu4{pc}"],
               lambda h: h.tensor_scalar(mu4, s4, 1.0 / D, None, ALU.mult))
            nmu4 = tiny_pool(f"F_nmu4{pc}")
            op("vector", [f"F_s4{pc}"], [f"F_nmu4{pc}"],
               lambda h: h.tensor_scalar(nmu4, s4, -1.0 / D, None, ALU.mult))
            v4 = tiny_pool(f"F_v4{pc}")
            op("scalar", [yn, f"F_nmu4{pc}"], [an, f"F{pc}_acc"],
               lambda h, pc=pc, pl=pl: h.activation(fA[pc], fYa[pl][:, :],
                                                    AF.Square, bias=nmu4,
                                                    accum_out=v4))
            r4 = rstd(v4, D, EPS_OUT, f"F{pc}")
            nm4 = tiny_pool(f"F_nm4{pc}")
            op("vector", [f"F_mu4{pc}", f"F{pc}_r"], [f"F_nm4{pc}"],
               lambda h: h.tensor_scalar(nm4, mu4, r4, -1.0, ALU.mult,
                                         ALU.mult))
            op("scalar", [yn, f"F{pc}_r", f"F_nm4{pc}"], [bn],
               lambda h, pc=pc, pl=pl: h.activation(fB[pc], fYa[pl][:, :],
                                                    AF.Identity, scale=r4,
                                                    bias=nm4))
            on = f"fOU{pc}"
            op("vector", [bn, x2n], [on],
               lambda h, pc=pc, pl=pl: h.tensor_tensor(
                   fOU[pc][:, :], fB[pc], fX2a[pl], ALU.add))
            dma("scalar", dou[pc], [on], [],
                out_t[b, i * 128:(i + 1) * 128, :], fOU[pc])

        pend = []

        def emit_load(t):
            b, i, _ = items_f[t]
            pl = t % 4
            dma("sync", dx[pl], [], [f"fX2{pl}"], fX2a[pl],
                spill[b, i * 128:(i + 1) * 128, :])

        def emit_clip(t):
            pc, pq = t % 2, t % 4
            v, nm = xnf(t)
            an = f"fA{pc}"
            op("vector", [nm, "s_g"], [an],
               lambda h, v=v, pc=pc: h.tensor_scalar(
                   fA[pc], v, s_g, CLIP_HI, ALU.mult, ALU.min))
            op("vector", [an], [f"fXQF{pq}"],
               lambda h, pc=pc, pq=pq: h.tensor_scalar(
                   fXQFa[pq], fA[pc], CLIP_LO, None, ALU.max))

        def cxb_load(t):
            b, i, _ = items_f[t]
            sl = t % 4
            dma("gpsimd", dxb[sl], [], [f"cXB{sl}"], cXB[sl],
                xnsp[b, i * 128:(i + 1) * 128, :])

        def cxb_tr(t):
            sl, pc = t % 4, t % 2
            vlt = vLT0 if pc == 0 else vLT1
            psn = f"psS{pc}"

            def tr(h, vlt=vlt, sl=sl):
                for c in range(NCH):
                    last = h.transpose(vlt[:, c * 128:(c + 1) * 128],
                                       cXB[sl][:, c * 128:(c + 1) * 128],
                                       idn_bf[:, :])
                return last

            op("tensor", [f"cXB{sl}", "idn_bf"], [psn], tr)
            v, nm = xnf(t)
            op("vector", [psn], [nm],
               lambda h, v=v, vlt=vlt: h.tensor_copy(v, vlt))

        for u in range(NPRE, min(NPRE + 4, NIT)):
            cxb_load(u)
        emit_load(0)
        emit_load(1)
        emit_clip(0)
        emit_clip(1)
        for b, i, t in items_f:
            pc, pq = t % 2, t % 4

            if len(pend) >= 2:
                chain_f(*pend.pop(0))

            psY, pyn = (psA, "psA") if pc == 0 else (psB, "psB")

            def ymm(h, psY=psY, pq=pq):
                for uh in range(2):
                    for dc in range(NCH):
                        last = h.matmul(
                            psY[:, uh * 512:(uh + 1) * 512],
                            fXQFa[pq][:, dc * 128:(dc + 1) * 128],
                            WQS[:, dc * D + uh * 512:dc * D + (uh + 1) * 512],
                            start=(dc == 0), stop=(dc == NCH - 1))
                return last

            op("tensor", [f"fXQF{pq}", "WQS"], [pyn], ymm)
            if t + 2 < NIT:
                emit_load(t + 2)
                emit_clip(t + 2)
            if NPRE <= t + 4 < NIT:
                cxb_tr(t + 4)
            if NPRE + 4 <= t + 8 < NIT:
                cxb_load(t + 8)
            yn = f"fY{pq}"
            s4 = tiny_pool(f"F_s4{pc}")
            op("scalar", [pyn, "gb_g"], [yn, f"F_s4{pc}"],
               lambda h, psY=psY, pq=pq, s4=s4: h.activation(
                   fYa[pq], psY[:, :], AF.Identity, scale=gb_g,
                   accum_out=s4))
            pend.append((b, i, t))

        for it in pend:
            chain_f(*it)

        def fin(h):
            h.wait_ge(dou[0].sem, dou[0].n * 16)
            h.wait_ge(dou[1].sem, dou[1].n * 16)

        sec("scalar", fin)

    return nc


def kernel(**inputs):
    x = np.ascontiguousarray(np.asarray(inputs["L0_attn_x"], dtype=np.float32))
    args = {
        "w1": np.ascontiguousarray(np.asarray(inputs["L0_attn_W1"], np.float32)),
        "w2": np.ascontiguousarray(np.asarray(inputs["L0_attn_W2"], np.float32)),
        "w3": np.ascontiguousarray(np.asarray(inputs["L0_attn_W3"], np.float32)),
        "w4": np.ascontiguousarray(np.asarray(inputs["L0_attn_W4"], np.float32)),
        "wf": np.ascontiguousarray(np.asarray(inputs["L0_ffn_weight"], np.float32)),
        "b1": np.ascontiguousarray(np.asarray(inputs["L0_attn_b1"], np.float32)),
        "b2": np.ascontiguousarray(np.asarray(inputs["L0_attn_b2"], np.float32)),
        "b3": np.ascontiguousarray(np.asarray(inputs["L0_attn_b3"], np.float32)),
        "b4": np.ascontiguousarray(np.asarray(inputs["L0_attn_b4"], np.float32)),
        "gam": np.ascontiguousarray(np.asarray(inputs["L0_ffn_ln_gamma"], np.float32)),
        "bet": np.ascontiguousarray(np.asarray(inputs["L0_ffn_ln_beta"], np.float32)),
    }
    n_cores = 8
    nb = x.shape[0] // n_cores
    nc = build_nc(B=nb, T=x.shape[1])
    in_maps = []
    for c in range(n_cores):
        m = {"x": np.ascontiguousarray(x[c * nb:(c + 1) * nb])}
        m.update(args)
        in_maps.append(m)
    res = run_bass_kernel_spmd(nc, in_maps, core_ids=list(range(n_cores)))
    return np.concatenate([r["out"] for r in res.results], axis=0)
